# version 26
# baseline (speedup 1.0000x reference)
import contextlib
import numpy as np
import ml_dtypes
import concourse.bass as bass
import concourse.mybir as mybir
from concourse.bass_utils import run_bass_kernel_spmd

F32 = mybir.dt.float32
BF16 = mybir.dt.bfloat16
AF = mybir.ActivationFunctionType
ALU = mybir.AluOpType
AX = mybir.AxisListType
ESZ = {F32: 4, BF16: 2}


class Cfg:
    def __init__(self, L=2048, NH=32, H=5632):
        self.L = L
        self.NH = NH
        self.HD = 64
        self.D = NH * self.HD
        self.H = H
        self.LT = L // 128
        self.DT = self.D // 128
        self.HT = H // 128
        self.OC = self.D // 512
        self.QB = L // 512


STREAM_OF_Q = {"sy": "sy", "sq": "act", "sw": "po"}
ENG_SEMS = ("pe", "act", "dve", "po")
ALL_STREAMS = ("sy", "act", "pe", "dve", "po")
POOL_N = {"sy": 48, "sw": 32, "sq": 8}


class Prog:

    def __init__(self, nc, es):
        self.nc = nc
        self._es = es
        self.items = {s: [] for s in ALL_STREAMS}
        self.sems = {}
        self.cnt = {}
        self.pending = {s: [] for s in ALL_STREAMS}
        self.touched = set()
        for s in ENG_SEMS:
            self.sems[s] = es.enter_context(nc.semaphore("S_" + s))
            self.cnt[s] = 0
        self.pool = {}
        self.pidx = {}
        for q, npool in POOL_N.items():
            self.pool[q] = []
            self.pidx[q] = 0
            for i in range(npool):
                name = f"{q}{i}"
                self.sems[name] = es.enter_context(nc.semaphore("S_" + name))
                self.cnt[name] = 0
                self.pool[q].append(name)

    def tok(self, sem):
        return (sem, self.cnt[sem])

    def emit(self, sem, fn, waits=(), n=1, stream=None):
        if stream is None:
            stream = sem
        ws = []
        for w in waits:
            if w is None:
                continue
            if isinstance(w, list):
                ws.extend(x for x in w if x is not None and x[1] > 0)
            elif w[1] > 0:
                ws.append(w)
        if self.pending[stream]:
            ws = self.pending[stream] + ws
            self.pending[stream] = []
        self.cnt[sem] += n
        self.touched.add(sem)
        self.items[stream].append((ws, fn, self.sems[sem], n))
        return (sem, self.cnt[sem])

    def dma_once(self, q, out, in_, waits=(), transpose=False):
        name = f"once{len([k for k in self.sems if k.startswith('once')])}"
        self.sems[name] = self._es.enter_context(self.nc.semaphore("S_" + name))
        self.cnt[name] = 0
        if transpose:
            fn = lambda e: e.dma_start(out=out, in_=in_, transpose=True)
        else:
            fn = lambda e: e.dma_start(out=out, in_=in_)
        return self.emit(name, fn, waits, n=16, stream=STREAM_OF_Q[q])

    def dma(self, q, out, in_, waits=(), transpose=False):
        assert q in ("sy", "sq", "sw")
        sem = self.pool[q][self.pidx[q]]
        self.pidx[q] = (self.pidx[q] + 1) % len(self.pool[q])
        if transpose:
            fn = lambda e: e.dma_start(out=out, in_=in_, transpose=True)
        else:
            fn = lambda e: e.dma_start(out=out, in_=in_)
        return self.emit(sem, fn, waits, n=16, stream=STREAM_OF_Q[q])

    def qtok(self, q):
        return [(s, self.cnt[s]) for s in self.pool[q] if self.cnt[s] > 0]

    def barrier(self):
        toks = [(s, self.cnt[s]) for s in self.touched if self.cnt[s] > 0]
        self.touched = set()
        for s in ALL_STREAMS:
            self.pending[s] = self.pending[s] + toks

    def replay(self, stream, eng):
        hi = {}
        for ws, fn, sem, n in self.items[stream]:
            for (wsem, wcnt) in ws:
                if wcnt > hi.get(wsem, 0):
                    eng.wait_ge(self.sems[wsem], wcnt)
                    hi[wsem] = wcnt
            ins = fn(eng)
            if n > 0:
                ins.then_inc(sem, n)


class Region:
    def __init__(self, nc, base, size):
        self.nc = nc
        self.base = base
        self.size = size
        self.cur = 0

    def take(self, name, sh, dt):
        nb = sh[1] * ESZ[dt]
        nb = (nb + 63) & ~63
        assert self.cur + nb <= self.size, f"{name}: region overflow {self.cur + nb} > {self.size}"
        t = self.nc.alloc_sbuf_tensor_at(name, sh, dt, offset=self.base + self.cur)
        self.cur += nb
        return t

    def reset(self):
        self.cur = 0


def build_nc(cfg: Cfg, dbg=False):
    L, D, H, NH, HD = cfg.L, cfg.D, cfg.H, cfg.NH, cfg.HD
    LT, DT, HT, OC, QB = cfg.LT, cfg.DT, cfg.HT, cfg.OC, cfg.QB
    nc = bass.Bass("TRN2", target_bir_lowering=False, detect_race_conditions=False)

    x_in = nc.dram_tensor("x", [L, D], F32, kind="ExternalInput")
    w_in = {m: nc.dram_tensor(m, [D, D], F32, kind="ExternalInput")
            for m in ("wq", "wk", "wv", "wo")}
    b_in = {m: nc.dram_tensor(m, [1, D], F32, kind="ExternalInput")
            for m in ("bq", "bk", "bv", "bo")}
    w1 = nc.dram_tensor("w1", [H, D], F32, kind="ExternalInput")
    w2 = nc.dram_tensor("w2", [H, D], F32, kind="ExternalInput")
    w3 = nc.dram_tensor("w3", [D, H], F32, kind="ExternalInput")
    cos_c = nc.dram_tensor("cos_c", [128, LT * NH], BF16, kind="ExternalInput")
    sin_c = nc.dram_tensor("sin_c", [128, LT * NH], BF16, kind="ExternalInput")
    ut_c = nc.dram_tensor("ut_c", [128, 896], BF16, kind="ExternalInput")
    id_c = nc.dram_tensor("id_c", [128, 128], BF16, kind="ExternalInput")
    out_t = nc.dram_tensor("out", [L, D], F32, kind="ExternalOutput")

    KI = "ExternalOutput" if dbg else "Internal"
    wbf = {m: nc.dram_tensor(f"{m}_bf", [D, D], BF16, kind=KI)
           for m in ("wq", "wk", "wv", "wo")}
    xnT_sc = nc.dram_tensor("xnT_sc", [D, L], BF16, kind=KI)
    hbm_p2 = nc.dram_tensor("hbm_p2", [6, 128], F32, kind="Internal")
    q_sc = nc.dram_tensor("q_sc", [L, D], BF16, kind=KI)
    k_sc = nc.dram_tensor("k_sc", [L, D], BF16, kind=KI)
    s_sc = nc.dram_tensor("s_sc", [L, D], BF16, kind=KI)
    w1q = nc.dram_tensor("w1q", [H, D], BF16, kind=KI)
    w2q = nc.dram_tensor("w2q", [H, D], BF16, kind=KI)
    w3q = nc.dram_tensor("w3q", [D, H], BF16, kind=KI)
    w1qT = nc.dram_tensor("w1qT", [H // 256, D * 256], BF16, kind=KI)
    w2qT = nc.dram_tensor("w2qT", [H // 256, D * 256], BF16, kind=KI)
    w3qT = nc.dram_tensor("w3qT", [D // 512, H * 512], BF16, kind=KI)
    ctx_dbg = nc.dram_tensor("ctx_dbg", [128, DT * L], BF16, kind=KI) if dbg else None

    es = contextlib.ExitStack()
    P = Prog(nc, es)

    def SB(n, sh, dt, st=es):
        return st.enter_context(nc.sbuf_tensor(n, sh, dt))

    def PSUM(n, sh, dt, st):
        return st.enter_context(nc.psum_tensor(n, sh, dt))

    TOP = nc.sbuf_top
    A_SZ = DT * L * 2
    B_SZ = max(HT * 1024 * 2, ((LT * NH * 65 * 2 + 63) & ~63) + 22528, 16 * D + 8 * H)
    P4RC = H * 2 + 1024 + 1024 + 1792 + 6 * L * 2
    C_SZ = max(max(2 * DT * 1024 + 2 * (LT * NH * 2) + 2048, P4RC)
               + 8192 + 4096 + 512,
               3 * DT * 512 + 3072,
               2 * HT * 512 + 7680)
    A_OFF = TOP - A_SZ
    B_OFF = A_OFF - B_SZ
    C_OFF = B_OFF - C_SZ
    nc.sbuf_top = C_OFF
    RA = Region(nc, A_OFF, A_SZ)
    RB = Region(nc, B_OFF, B_SZ)
    RC = Region(nc, C_OFF, C_SZ)

    id_sb = SB("id_sb", [128, 128], BF16)
    ones_r = SB("ones_r", [1, 128], BF16)
    ones_rf = SB("ones_rf", [1, 128], F32)
    ones_cf = SB("ones_cf", [128, 1], F32)
    GWc = min(1024, D)
    NGJ = 2 * (H // 128) * ((D + GWc - 1) // GWc) + (D // 128) * ((H + GWc - 1) // GWc)
    gacc = SB("gacc", [128, NGJ + 4], F32)
    th_t = SB("th_t", [1, 4], F32)
    thv = SB("thv", [128, 3], F32)
    nthv = SB("nthv", [128, 3], F32)
    qn2m = SB("qn2m", [128, NH], F32)
    kn2m = SB("kn2m", [128, NH], F32)
    negm = SB("negm", [128, NH], F32)
    nrm8 = SB("nrm8", [128, 8], F32)
    stat = SB("stat", [128, 20], F32)
    ones64 = SB("ones64", [128, 64], BF16)

    t_id = P.dma("sy", id_sb[:, :], id_c.ap())
    t_cast = {}
    for m in ("wq", "wk", "wv", "wo"):
        t_cast[m] = P.dma_once("sw", wbf[m].ap(), w_in[m].ap())
    t_ones = P.emit("dve", lambda e: e.memset(ones_r[:, :], 1.0))
    t_ones2 = P.emit("dve", lambda e: e.memset(ones_rf[:, :], 1.0))
    t_ones3 = P.emit("dve", lambda e: e.memset(ones_cf[:, :], 1.0))
    t_o64 = P.emit("dve", lambda e: e.memset(ones64[64:65, :], 1.0))
    t_qn0 = P.emit("dve", lambda e: e.memset(qn2m[:, :], 0.0))
    t_kn0 = P.emit("dve", lambda e: e.memset(kn2m[:, :], 0.0))

    XNT = RA.take("XNT", [128, DT * L], BF16)
    xa = [RB.take("xa0", [128, D], F32), RB.take("xa1", [128, D], F32)]
    sqb = RB.take("sqb", [128, D], F32)
    xcb = [RB.take("xcb0", [128, D], BF16), RB.take("xcb1", [128, D], BF16)]
    p1_end = RB.cur
    GW = min(1024, D)
    gch = [nc.alloc_sbuf_tensor_at("gch0", [128, GW], F32,
                                   offset=B_OFF + B_SZ - 2 * GW * 4),
           nc.alloc_sbuf_tensor_at("gch1", [128, GW], F32,
                                   offset=B_OFF + B_SZ - GW * 4)]
    es1p = contextlib.ExitStack()
    pstr = [PSUM("pstr0", [128, 1024], BF16, es1p),
            PSUM("pstr1", [128, 1024], BF16, es1p)]

    gamma_jobs = []
    gbnd = []
    for mi, (wsrc, rows, width) in enumerate(((w1, H, D), (w2, H, D), (w3, D, H))):
        for r0 in range(0, rows, 128):
            for c0 in range(0, width, GW):
                cw = min(GW, width - c0)
                gamma_jobs.append((wsrc.ap()[r0:r0 + 128, c0:c0 + cw], cw))
        gbnd.append(len(gamma_jobs))
    gj_i = 0
    t_gred = [None, None]
    t_gam_last = None
    t_gam_m = [None, None, None]

    def emit_gamma_some(k):
        nonlocal gj_i, t_gam_last
        for _ in range(k):
            if gj_i >= len(gamma_jobs):
                return
            src, cw = gamma_jobs[gj_i]
            b = gj_i % 2
            tl = P.dma("sw", gch[b][:, 0:cw], src, waits=[t_gred[b]])
            t_gred[b] = P.emit("dve", (lambda bb, cc, ww: lambda e: e.tensor_reduce(
                gacc[:, cc:cc + 1], gch[bb][:, 0:ww], AX.X, ALU.add,
                apply_absolute_value=True))(b, gj_i, cw), waits=[tl])
            t_gam_last = t_gred[b]
            gj_i += 1
            for m_ in range(3):
                if gj_i == gbnd[m_]:
                    t_gam_m[m_] = t_gam_last

    NHALF = max(1, DT // 8)
    t_xa_free = [None, None]
    t_xcb_free = [None, None]
    t_sqb_free = None
    t_pstr_free = [None, None]
    t_xnt_w = [None] * LT
    t_xnt_ready = [None] * LT
    t_ln_last = []
    invD = 1.0 / D

    def emit_ln(lt):
        nonlocal t_sqb_free
        b = lt % 2
        sb_ = b * 10
        t_x = P.dma("sy", xa[b][:, :], x_in.ap()[lt * 128:(lt + 1) * 128, :],
                    waits=[t_xa_free[b]])
        t_sq = P.emit("act", (lambda bb: lambda e: e.activation(
            sqb[:, :], xa[bb][:, :], AF.Square))(b), waits=[t_x, t_sqb_free])
        t_s1 = P.emit("dve", (lambda bb, c: lambda e: e.tensor_reduce(
            stat[:, c:c + 1], xa[bb][:, :], AX.X, ALU.add))(b, sb_), waits=[t_x])
        t_s2 = P.emit("dve", (lambda c: lambda e: e.tensor_reduce(
            stat[:, c + 1:c + 2], sqb[:, :], AX.X, ALU.add))(sb_), waits=[t_sq])
        t_sqb_free = t_s2
        t_mu = P.emit("dve", (lambda c: lambda e: e.tensor_scalar_mul(
            stat[:, c + 2:c + 3], stat[:, c:c + 1], invD))(sb_), waits=[t_s1])
        t_m2 = P.emit("dve", (lambda c: lambda e: e.tensor_scalar_mul(
            stat[:, c + 3:c + 4], stat[:, c + 1:c + 2], invD))(sb_), waits=[t_s2])
        t_mm2 = P.emit("dve", (lambda c: lambda e: e.tensor_mul(
            stat[:, c + 4:c + 5], stat[:, c + 2:c + 3], stat[:, c + 2:c + 3]))(sb_),
            waits=[t_mu])
        t_var = P.emit("dve", (lambda c: lambda e: e.tensor_sub(
            stat[:, c + 5:c + 6], stat[:, c + 3:c + 4], stat[:, c + 4:c + 5]))(sb_),
            waits=[t_m2, t_mm2])
        t_ve = P.emit("dve", (lambda c: lambda e: e.tensor_scalar_add(
            stat[:, c + 6:c + 7], stat[:, c + 5:c + 6], 1e-5))(sb_), waits=[t_var])
        t_sd = P.emit("act", (lambda c: lambda e: e.activation(
            stat[:, c + 7:c + 8], stat[:, c + 6:c + 7], AF.Sqrt))(sb_), waits=[t_ve])
        t_rs = P.emit("dve", (lambda c: lambda e: e.reciprocal(
            stat[:, c + 8:c + 9], stat[:, c + 7:c + 8]))(sb_), waits=[t_sd])
        t_xc = P.emit("dve", (lambda bb, c: lambda e: e.tensor_scalar(
            xcb[bb][:, :], xa[bb][:, :], stat[:, c + 2:c + 3], stat[:, c + 8:c + 9],
            ALU.subtract, ALU.mult))(b, sb_), waits=[t_rs, t_xcb_free[b]])
        t_xa_free[b] = [t_xc, t_sq]
        t_tr_last = None
        t_ev = None
        for half in range(NHALF):
            nsub = min(8, DT - half * 8)
            hb = half % 2
            pb = pstr[hb]
            tm = None
            for j in range(nsub):
                dt = half * 8 + j
                tm = P.emit("pe", (lambda bb, dtt, jj, pbb: lambda e: e.transpose(
                    pbb[:, jj * 128:(jj + 1) * 128],
                    xcb[bb][:, dtt * 128:(dtt + 1) * 128], id_sb[:, :]))(b, dt, j, pb),
                    waits=[t_xc, t_id, t_pstr_free[hb]])
            t_tr_last = tm
            xv = bass.AP(XNT, (half * 8) * L + lt * 128,
                         [[DT * L, 128], [L, nsub], [1, 128]])
            pv = bass.AP(pb, 0, [[1024, 128], [128, nsub], [1, 128]])
            t_ev = P.emit("dve", (lambda xvv, pvv: lambda e: e.tensor_copy(xvv, pvv))(
                xv, pv), waits=[tm])
            t_pstr_free[hb] = t_ev
        t_xcb_free[b] = t_tr_last
        xsrc = bass.AP(XNT, lt * 128, [[DT * L, 128], [L, DT], [1, 128]])
        xdst = bass.AP(xnT_sc, lt * 128, [[L, 128], [128 * L, DT], [1, 128]])
        t_xnt_w[lt] = P.dma("sw", xdst, xsrc, waits=[t_ev])
        t_xnt_ready[lt] = P.tok("dve")
        t_ln_last[:] = [t_xc, t_sq]

    p2sc = nc.alloc_sbuf_tensor_at("p2sc", [1, 384], F32,
                                   offset=B_OFF + B_SZ - 2 * GW * 4 - 2048)
    ones1 = nc.alloc_sbuf_tensor_at("ones1", [1, 128], F32,
                                    offset=B_OFF + B_SZ - 2 * GW * 4 - 512)
    t_thv = []
    p2_state = {"ones": None, "rd": None, "bc": None}

    def emit_p2_m(mi):
        rows, width = ((H, D), (H, D), (D, H))[mi]
        c0 = 0 if mi == 0 else gbnd[mi - 1]
        c1 = gbnd[mi]
        if p2_state["ones"] is None:
            p2_state["ones"] = P.emit("dve", lambda e: e.memset(ones1[0:1, :], 1.0))
        t_csl = P.emit("dve", (lambda a0, a1, m: lambda e: e.tensor_reduce(
            gacc[:, NGJ + m:NGJ + m + 1], gacc[:, a0:a1], AX.X, ALU.add))(
                c0, c1, mi), waits=[t_gam_m[mi]])
        t_st1 = P.dma("sw", hbm_p2.ap()[mi:mi + 1, 0:128],
                      gacc[:, NGJ + mi:NGJ + mi + 1], waits=[t_csl])
        t_ld1 = P.dma("sw", p2sc[0:1, 0:128], hbm_p2.ap()[mi:mi + 1, 0:128],
                      waits=[t_st1, p2_state["rd"]])
        t_rd = P.emit("dve", (lambda m: lambda e: e.tensor_reduce(
            p2sc[0:1, 256 + m:257 + m], p2sc[0:1, 0:128], AX.X, ALU.add))(mi),
            waits=[t_ld1])
        p2_state["rd"] = t_rd
        t_th = P.emit("dve", (lambda m, ne: lambda e: e.tensor_scalar(
            th_t[0:1, m:m + 1], p2sc[0:1, 256 + m:257 + m], 0.5 / ne, 0.5e-8,
            ALU.mult, ALU.add))(mi, rows * width), waits=[t_rd])
        t_bc = P.emit("dve", (lambda m: lambda e: e.tensor_scalar(
            p2sc[0:1, 128:256], ones1[0:1, :], th_t[0:1, m:m + 1], 0.0,
            ALU.mult, ALU.add))(mi),
            waits=[t_th, p2_state["ones"], p2_state["bc"]])
        t_st2 = P.dma("sw", hbm_p2.ap()[3 + mi:4 + mi, 0:128],
                      p2sc[0:1, 128:256], waits=[t_bc])
        p2_state["bc"] = t_st2
        t_ld2 = P.dma("sw", thv[:, mi:mi + 1], hbm_p2.ap()[3 + mi:4 + mi, 0:128],
                      waits=[t_st2])
        t_nv = P.emit("dve", (lambda m: lambda e: e.tensor_scalar_mul(
            nthv[:, m:m + 1], thv[:, m:m + 1], -1.0))(mi), waits=[t_ld2])
        t_thv.append((t_ld2, t_nv))

    RB.reset()
    VBUF = RB.take("VBUF", [128, LT * NH * 65], BF16)
    RB.cur = max(RB.cur, p1_end)
    vb_save = RB.cur
    qe = [RB.take("qe0", [128, 512], BF16), RB.take("qe1", [128, 512], BF16)]
    rtmp = RB.take("rtmp", [128, 512], BF16)
    qr = [RB.take("qr0", [128, 512], BF16), RB.take("qr1", [128, 512], BF16)]
    sqf = RB.take("sqf", [128, 512], BF16)
    rtmp2 = RB.take("rtmp2", [128, 512], BF16)
    RC.reset()
    wXT = [RC.take("wXT0", [128, DT * 512], BF16), RC.take("wXT1", [128, DT * 512], BF16)]
    cos_sb = RC.take("cos_sb", [128, LT * NH], BF16)
    sin_sb = RC.take("sin_sb", [128, LT * NH], BF16)
    bias_sl = [RC.take("bias_sl0", [1, 512], BF16), RC.take("bias_sl1", [1, 512], BF16)]
    RC.cur = max(RC.cur, P4RC)
    dch = [RC.take("dch0", [128, 1024], F32), RC.take("dch1", [128, 1024], F32)]
    q2a = RC.take("q2a", [128, 1024], BF16)
    q2cb = RC.take("q2c", [128, 1024], BF16)
    t_cos = P.dma("sy", cos_sb[:, :], cos_c.ap())
    t_sin = P.dma("sy", sin_sb[:, :], sin_c.ap())
    es3p = contextlib.ExitStack()
    pqkv = [PSUM(f"pqkv{i}", [128, 512], F32, es3p) for i in range(3)]

    QW = min(1024, D)
    quant_jobs = []
    for mi, (wsrc, wdst, rows) in enumerate(((w1, w1q, H), (w2, w2q, H))):
        for r0 in range(0, rows, 128):
            for c0 in range(0, D, QW):
                quant_jobs.append((mi, wsrc.ap()[r0:r0 + 128, c0:c0 + QW],
                                   wdst.ap()[r0:r0 + 128, c0:c0 + QW]))
    qj_i = 0
    ql_i = 0
    t_dch_free = [None, None]
    t_dch_load = [None, None]
    t_q2c_free = None
    t_q2a_free = None
    t_qdone = {0: None, 1: None}

    def emit_quant_some(k):
        nonlocal qj_i, ql_i, t_q2c_free, t_q2a_free
        for _ in range(k):
            if qj_i >= len(quant_jobs):
                return
            while ql_i < min(qj_i + 2, len(quant_jobs)):
                mi_, src_, dst_ = quant_jobs[ql_i]
                db_ = ql_i % 2
                t_dch_load[db_] = P.dma("sw", dch[db_][:, 0:QW], src_,
                                        waits=[t_dch_free[db_]])
                ql_i += 1
            mi, src, dst = quant_jobs[qj_i]
            db = qj_i % 2
            tl = t_dch_load[db]
            ts1 = P.emit("act", (lambda m, d_: lambda e: e.activation(
                q2a[:, 0:QW], dch[d_][:, 0:QW], AF.Sign, bias=nthv[:, m:m + 1]))(
                    mi, db), waits=[tl, t_thv[mi][1], t_q2a_free])
            tb = P.emit("dve", (lambda m, d_: lambda e: e.tensor_scalar(
                q2cb[:, 0:QW], dch[d_][:, 0:QW], nthv[:, m:m + 1], 2.0,
                ALU.is_ge, ALU.mult))(mi, db),
                waits=[tl, t_thv[mi][1], t_q2c_free])
            t_dch_free[db] = [ts1, tb]
            tq = P.emit("dve", lambda e: e.scalar_tensor_tensor(
                q2cb[:, 0:QW], q2cb[:, 0:QW], -1.0, q2a[:, 0:QW],
                ALU.add, ALU.add), waits=[ts1, tb])
            t_q2a_free = tq
            tw = P.dma("sw", dst, q2cb[:, 0:QW], waits=[tq])
            t_q2c_free = tw
            if qj_i + 1 >= len(quant_jobs) or quant_jobs[qj_i + 1][0] != mi:
                t_qdone[mi] = P.qtok("sw")
            qj_i += 1

    t_wXT_free = [None, None]
    t_qe_free = [None, None]
    t_rtmp_free = [None, None]
    t_rtmp2_free = [None, None]
    t_qr_free = [None, None]
    t_sqf_free = None
    t_nrm8_free = None
    t_vb_done = None
    t_norm_acc = {"q": None, "k": None}
    t_ps_free = [None, None, None]
    t_bias = [None, None]
    bias_src = {"q": b_in["bq"], "k": b_in["bk"], "v": b_in["bv"]}
    psi = 0
    bsl_i = 0
    for mat in ("q", "k", "v"):
        for oc in range(OC):
            wb = ("qkv".index(mat) * OC + oc) % 2
            tws = []
            for dt in range(DT):
                tws.append(P.dma(
                    "sy", wXT[wb][:, dt * 512:(dt + 1) * 512],
                    wbf["w" + mat].ap()[oc * 512:(oc + 1) * 512, dt * 128:(dt + 1) * 128],
                    waits=[t_wXT_free[wb], t_cast["w" + mat]]
                    + (t_ln_last if mat == "v" and oc == 0 else []),
                    transpose=True))
            t_w_all = list(tws)
            bsl = bsl_i % 2
            bsl_i += 1
            t_bsl = P.dma("sw", bias_sl[bsl][0:1, :],
                          bias_src[mat].ap()[0:1, oc * 512:(oc + 1) * 512],
                          waits=[t_bias[bsl]])
            gi0 = "qkv".index(mat) * OC + oc
            t_last_mm = None
            for lt in range(LT):
                if gi0 == 0:
                    emit_ln(lt)
                myps = psi % 3
                ps = pqkv[myps]
                psi += 1
                mm_waits = t_w_all + [t_ps_free[myps], t_xnt_ready[lt]]
                for dt in range(DT):
                    P.emit("pe", (lambda pp, dtt, ltt, wbb: lambda e: e.matmul(
                        pp[:, :], XNT[:, dtt * L + ltt * 128:dtt * L + ltt * 128 + 128],
                        wXT[wbb][:, dtt * 512:(dtt + 1) * 512],
                        start=(dtt == 0), stop=False))(ps, dt, lt, wb),
                        waits=(mm_waits if dt == 0 else []), n=0)
                t_mm = P.emit("pe", (lambda pp, bb: lambda e: e.matmul(
                    pp[:, :], ones_r[:, :], bias_sl[bb][0:1, :],
                    start=False, stop=True))(ps, bsl), waits=[t_ones, t_bsl])
                t_last_mm = t_mm
                if mat in ("q", "k"):
                    qb_ = lt % 2
                    myqe = qe[lt % 2]
                    t_qev = P.emit("act", (lambda pp, qq: lambda e: e.copy(
                        qq[:, :], pp[:, :]))(ps, myqe),
                        waits=[t_mm, t_qe_free[lt % 2]])
                    t_ps_free[myps] = t_qev
                    co = bass.AP(cos_sb, lt * NH + oc * 8,
                                 [[LT * NH, 128], [1, 8], [0, 32]])
                    si = bass.AP(sin_sb, lt * NH + oc * 8,
                                 [[LT * NH, 128], [1, 8], [0, 32]])
                    qeA = bass.AP(myqe, 0, [[512, 128], [64, 8], [1, 32]])
                    qeB = bass.AP(myqe, 32, [[512, 128], [64, 8], [1, 32]])
                    tA = bass.AP(rtmp, 0, [[512, 128], [32, 8], [1, 32]])
                    tB = bass.AP(rtmp, 256, [[512, 128], [32, 8], [1, 32]])
                    qrA = bass.AP(qr[qb_], 0, [[512, 128], [64, 8], [1, 32]])
                    qrB = bass.AP(qr[qb_], 32, [[512, 128], [64, 8], [1, 32]])
                    tC = bass.AP(rtmp2, 0, [[512, 128], [32, 8], [1, 32]])
                    tD = bass.AP(rtmp2, 256, [[512, 128], [32, 8], [1, 32]])
                    TT = lambda a1, a2, a3, op: (lambda e: e.tensor_tensor(a1, a2, a3, op))
                    w1_ = P.emit("dve", TT(tA, qeA, co, ALU.mult),
                                 waits=[t_qev, t_cos, t_rtmp_free[0]])
                    w2_ = P.emit("dve", TT(tB, qeB, si, ALU.mult),
                                 waits=[t_sin, t_rtmp_free[1]])
                    w3_ = P.emit("dve", TT(qrA, tA, tB, ALU.subtract),
                                 waits=[w1_, w2_, t_qr_free[qb_]])
                    w4_ = P.emit("po", TT(tC, qeB, co, ALU.mult),
                                 waits=[t_qev, t_cos, t_rtmp2_free[0]])
                    w5_ = P.emit("po", TT(tD, qeA, si, ALU.mult),
                                 waits=[t_rtmp2_free[1]])
                    w6_ = P.emit("po", TT(qrB, tC, tD, ALU.add),
                                 waits=[w4_, w5_, t_qr_free[qb_]])
                    t_qe_free[lt % 2] = [w2_, w5_]
                    t_rtmp_free = [w3_, w3_]
                    t_rtmp2_free = [w6_, w6_]
                    t_sq2 = P.emit("act", (lambda qb2: lambda e: e.activation(
                        sqf[:, :], qr[qb2][:, :], AF.Square))(qb_),
                        waits=[w3_, w6_, t_sqf_free])
                    sqr = sqf[:, :].rearrange("p (n j) -> p n j", j=64)
                    t_red = P.emit("dve", (lambda sr: lambda e: e.tensor_reduce(
                        nrm8[:, 0:8], sr, AX.X, ALU.add))(sqr),
                        waits=[t_sq2, t_nrm8_free])
                    t_sqf_free = t_red
                    nacc = qn2m if mat == "q" else kn2m
                    t_nx = P.emit("dve", (lambda na, occ: lambda e: e.tensor_tensor(
                        na[:, occ * 8:occ * 8 + 8], na[:, occ * 8:occ * 8 + 8],
                        nrm8[:, 0:8], ALU.max))(nacc, oc),
                        waits=[t_red, t_norm_acc[mat], t_qn0, t_kn0])
                    t_nrm8_free = t_nx
                    t_norm_acc[mat] = t_nx
                    dst_sc = q_sc if mat == "q" else k_sc
                    t_st = P.dma("sw", dst_sc.ap()[lt * 128:(lt + 1) * 128,
                                                   oc * 512:(oc + 1) * 512],
                                 qr[qb_][:, :], waits=[w3_, w6_])
                    t_qr_free[qb_] = [t_st, t_sq2]
                else:
                    vdst = bass.AP(VBUF, lt * (NH * 65) + oc * 8 * 65,
                                   [[LT * NH * 65, 128], [65, 8], [1, 64]])
                    t_vev = P.emit("dve", (lambda pp, vd: lambda e: e.tensor_copy(
                        vd, pp[:, :]))(ps, vdst), waits=[t_mm])
                    t_ps_free[myps] = t_vev
            t_wXT_free[wb] = t_last_mm
            t_bias[bsl] = t_last_mm
            pass
    for lt in range(LT):
        vd = bass.AP(VBUF, lt * (NH * 65) + 64, [[LT * NH * 65, 128], [65, NH], [1, 1]])
        t_vb_done = P.emit("dve", (lambda v: lambda e: e.memset(v, 1.0))(vd))

    es3n = contextlib.ExitStack()
    pnm1 = PSUM("pnm1", [NH, 128], BF16, es3n)
    pnm2 = PSUM("pnm2", [1, NH], BF16, es3n)
    pnm3 = PSUM("pnm3", [128, NH], F32, es3n)
    nq_b = SB("nq_b", [128, NH], BF16, es3n)
    nk_b = SB("nk_b", [128, NH], BF16, es3n)
    mx = SB("mx", [NH, 4], F32, es3n)
    nmb = SB("nmb", [NH, 1], BF16, es3n)
    nrow = SB("nrow", [1, NH], BF16, es3n)
    t_cq = P.emit("dve", lambda e: e.tensor_copy(nq_b[:, :], qn2m[:, :]),
                  waits=[t_norm_acc["q"]])
    t_ck = P.emit("dve", lambda e: e.tensor_copy(nk_b[:, :], kn2m[:, :]),
                  waits=[t_norm_acc["k"]])
    t_pn_free = None
    t_sq_toks = []
    for i, (nb, tn) in enumerate(((nq_b, t_cq), (nk_b, t_ck))):
        t_tr1 = P.emit("pe", (lambda nbb: lambda e: e.transpose(
            pnm1[:, :], nbb[:, :], id_sb[:, :]))(nb), waits=[tn, t_id, t_pn_free])
        t_mx = P.emit("dve", (lambda ii: lambda e: e.tensor_reduce(
            mx[:, ii:ii + 1], pnm1[:, :], AX.X, ALU.max))(i), waits=[t_tr1])
        t_pn_free = t_mx
        t_sqr = P.emit("act", (lambda ii: lambda e: e.activation(
            mx[:, 2 + ii:3 + ii], mx[:, ii:ii + 1], AF.Sqrt))(i), waits=[t_mx])
        t_sq_toks.append(t_sqr)
    t_mm_ = P.emit("dve", lambda e: e.tensor_mul(mx[:, 0:1], mx[:, 2:3], mx[:, 3:4]),
                   waits=t_sq_toks)
    t_nm = P.emit("dve", lambda e: e.tensor_scalar_mul(mx[:, 1:2], mx[:, 0:1], -0.125),
                  waits=[t_mm_])
    t_nmb = P.emit("dve", lambda e: e.tensor_copy(nmb[:, :], mx[:, 1:2]), waits=[t_nm])
    t_tr2 = P.emit("pe", lambda e: e.transpose(pnm2[:, :], nmb[:, :],
                                               id_sb[0:NH, 0:NH]), waits=[t_nmb])
    t_nrow = P.emit("dve", lambda e: e.tensor_copy(nrow[:, :], pnm2[:, :]),
                    waits=[t_tr2])
    t_bc = P.emit("pe", lambda e: e.matmul(pnm3[:, :], ones_r[:, :], nrow[:, :],
                                           start=True, stop=True),
                  waits=[t_nrow, t_ones])
    t_negm = P.emit("dve", lambda e: e.tensor_copy(negm[:, :], pnm3[:, :]),
                    waits=[t_bc])

    P.barrier()
    es3n.close()
    es3p.close()
    es1p.close()
    RA.reset()
    CTX = RA.take("CTX", [128, DT * L], BF16)
    RB.cur = vb_save
    expb = [RB.take(f"expb{i}", [128, 1024], BF16) for i in range(3)]
    RC.reset()
    H2 = H // 2
    rpb = [RC.take(f"rpb_{i}", [128, H2], BF16) for i in range(2)]
    rcs = RC.take("rcs", [128, 512], BF16)
    bcs = RC.take("bcs", [64, 512], BF16)
    ut_sb = RC.take("ut_sb", [128, 896], BF16)
    t_ut = P.dma("sy", ut_sb[:, :], ut_c.ap())
    qtb = [RC.take(f"qtb{i}", [128, L], BF16) for i in range(3)]
    ktb = [RC.take(f"ktb{i}", [128, L], BF16) for i in range(3)]
    W3W = H // 4
    w3ch_rel = RB.cur
    w3ch = RB.take("w3ch", [128, W3W], F32)
    w3qa = RB.take("w3qa", [128, W3W], BF16)
    w3qc = RB.take("w3qc", [128, W3W], BF16)
    HF3 = 2 if D >= 1024 else 1
    D2 = D // HF3
    rpb3 = [RB.take(f"rpb3_{i}", [128, D2], BF16) for i in range(2)]
    es4p = contextlib.ExitStack()
    psS = [PSUM(f"psS{i}", [128, 1024], F32, es4p) for i in range(3)]
    psc = PSUM("psc", [65, 512], F32, es4p)
    psbc = PSUM("psbc", [64, 512], F32, es4p)

    w3jobs = [("q", r0, c0) for r0 in range(0, D, 128)
              for c0 in range(0, H, H // 4)] + [
        ("r", ht, hf) for ht in range(HT) for hf in range(HF3)]
    w3_i = 0
    w3r_i = 0
    t_w3ch_free = None
    t_w3q_done = None
    t_w3a_free = None
    t_w3c_free = None
    t_rp3_free = [None, None]
    t_w3T_all = [None]

    def emit_w3_some(k):
        nonlocal w3_i, w3r_i, t_w3ch_free, t_w3q_done, t_w3a_free, t_w3c_free
        for _ in range(k):
            if w3_i >= len(w3jobs):
                return
            kind, idx, cc0 = w3jobs[w3_i]
            if kind == "r" and t_w3q_done is None:
                pass
            if kind == "q":
                tl = P.dma("sw", w3ch[:, :], w3.ap()[idx:idx + 128, cc0:cc0 + H // 4],
                           waits=[t_w3ch_free])
                ts1 = P.emit("act", lambda e: e.activation(
                    w3qa[:, :], w3ch[:, :], AF.Sign, bias=nthv[:, 2:3]),
                    waits=[tl, t_thv[2][1], t_w3a_free])
                tb = P.emit("dve", lambda e: e.tensor_scalar(
                    w3qc[:, :], w3ch[:, :], nthv[:, 2:3], 2.0,
                    ALU.is_ge, ALU.mult),
                    waits=[tl, t_thv[2][1], t_w3c_free])
                t_w3ch_free = [ts1, tb]
                tq = P.emit("dve", lambda e: e.scalar_tensor_tensor(
                    w3qc[:, :], w3qc[:, :], -1.0, w3qa[:, :],
                    ALU.add, ALU.add), waits=[ts1, tb])
                t_w3a_free = tq
                tw = P.dma("sw", w3q.ap()[idx:idx + 128, cc0:cc0 + H // 4],
                           w3qc[:, :], waits=[tq])
                t_w3c_free = tw
                if w3_i + 1 < len(w3jobs) and w3jobs[w3_i + 1][0] == "r":
                    t_w3q_done = P.qtok("sw")
            else:
                ht = idx
                hf = cc0
                b3 = w3r_i % 2
                w3r_i += 1
                tl = P.dma("sy", rpb3[b3][:, 0:D2],
                           w3q.ap()[hf * D2:(hf + 1) * D2,
                                    ht * 128:(ht + 1) * 128],
                           waits=[t_rp3_free[b3], t_w3q_done], transpose=True)
                dst = bass.AP(w3qT, ht * 128 * 512 + hf * (D2 // 512) * (H * 512),
                              [[512, 128], [H * 512, D2 // 512], [1, 512]])
                src = bass.AP(rpb3[b3], 0, [[D2, 128], [512, D2 // 512], [1, 512]])
                tw = P.dma("sw", dst, src, waits=[tl])
                t_rp3_free[b3] = tw
            w3_i += 1
            if w3_i == len(w3jobs):
                t_w3T_all[0] = P.qtok("sw")

    repack_jobs = []
    for (qsrc, qdst, mi) in ((w1q, w1qT, 0), (w2q, w2qT, 1)):
        for dt in range(DT):
            for hf in range(2):
                repack_jobs.append((qsrc, qdst, mi, dt, hf))
    rp_i = 0
    t_rpb_free = [None, None]
    t_rp_done = [None, None]

    def emit_repack_some(k):
        nonlocal rp_i
        for _ in range(k):
            if rp_i >= len(repack_jobs):
                return
            qsrc, qdst, mi, dt, hf = repack_jobs[rp_i]
            b = rp_i % 2
            tl = P.dma("sy", rpb[b][:, 0:H2],
                       qsrc.ap()[hf * H2:(hf + 1) * H2,
                                 dt * 128:(dt + 1) * 128],
                       waits=[t_rpb_free[b], t_qdone[mi]], transpose=True)
            dst = bass.AP(qdst, dt * 128 * 256 + hf * (H2 // 256) * (D * 256),
                          [[256, 128], [D * 256, H2 // 256], [1, 256]])
            src = bass.AP(rpb[b], 0, [[H2, 128], [256, H2 // 256], [1, 256]])
            tw = P.dma("sw", dst, src, waits=[tl])
            t_rpb_free[b] = tw
            if rp_i + 1 >= len(repack_jobs) or repack_jobs[rp_i + 1][2] != mi:
                t_rp_done[mi] = P.qtok("sw")
            rp_i += 1

    p2m_done = [False, False, False]

    def emit_tail(k):
        left = k
        while left > 0:
            for mi in range(3):
                if not p2m_done[mi] and gj_i >= gbnd[mi]:
                    emit_p2_m(mi)
                    p2m_done[mi] = True
            if gj_i < len(gamma_jobs):
                emit_gamma_some(1)
                left -= 1
                if (left > 0 and qj_i < len(quant_jobs)
                        and p2m_done[quant_jobs[qj_i][0]]):
                    emit_quant_some(1)
                    left -= 1
                continue
            if qj_i < len(quant_jobs):
                if not p2m_done[quant_jobs[qj_i][0]]:
                    return
                n0 = qj_i
                emit_quant_some(left)
                left -= qj_i - n0
                continue
            if rp_i < len(repack_jobs):
                n0 = rp_i
                emit_repack_some(left)
                left -= rp_i - n0
                continue
            if w3_i < len(w3jobs):
                if not p2m_done[2]:
                    return
                n0 = w3_i
                emit_w3_some(left)
                left -= w3_i - n0
                continue
            return

    t_qt = [None, None, None]
    t_kt = [None, None, None]
    t_qtb_free = [None, None, None]
    t_ktb_free = [None, None, None]
    last_sc = {}
    tasks = []
    for pm in range(NH // 2):
        for ni in range(2):
            n = 2 * pm + ni
            for qb in range(QB):
                nkt = (qb + 1) * 4
                for kt0 in range(0, nkt, 2):
                    tasks.append((n, pm % 3, ni * 64, qb, kt0, 2,
                                  kt0 + 2 == nkt))
    NT = len(tasks)
    t_exp_t = [None] * NT
    t_mask_t = [None] * NT
    t_sc_t = [None] * NT
    t_buf_prev = [[None, None] for _ in range(3)]
    t_psc_free = None
    t_psbc_free = None
    t_rcs_free = None
    t_bcs_free = None
    t_ctx_done = None
    seen_pm = -1

    def emit_qk_loads(pm, w):
        hb2 = pm % 3
        t_qt[hb2] = P.dma("sy", qtb[hb2][:, :],
                          q_sc.ap()[:, pm * 128:(pm + 1) * 128],
                          waits=w, transpose=True)
        t_kt[hb2] = P.dma("sy", ktb[hb2][:, :],
                          k_sc.ap()[:, pm * 128:(pm + 1) * 128],
                          waits=w, transpose=True)

    emit_qk_loads(0, [])

    def emit_scores_exp_mask(gi):
        nonlocal seen_pm, t_qt, t_kt
        (n, hb, rb, qb, g0, glen, lastq) = tasks[gi]
        pm = n // 2
        if pm > seen_pm:
            seen_pm = pm
            if pm >= 1:
                last_sc[pm - 1] = t_sc_t[gi - 1]
            if pm + 1 < NH // 2:
                emit_qk_loads(pm + 1, [last_sc.get(pm - 2)])
        bi = gi % 3
        pss = psS[bi]
        exb = expb[bi]
        prev_exp, prev_ctx = t_buf_prev[bi]
        t_sc = None
        for j in range(glen):
            kt = g0 + j
            t_sc = P.emit("pe", (lambda ps_, jj, ktt, hbb, qbb, rbb:
                                 lambda e: e.matmul(
                ps_[:, jj * 512:(jj + 1) * 512],
                ktb[hbb][rbb:rbb + 64, ktt * 128:ktt * 128 + 128],
                qtb[hbb][rbb:rbb + 64, qbb * 512:(qbb + 1) * 512],
                start=True, stop=True))(pss, j, kt, hb, qb, rb),
                waits=[t_qt[hb], t_kt[hb], prev_exp],
                n=(1 if j == glen - 1 else 0))
        t_sc_t[gi] = t_sc
        t_exp = P.emit("act", (lambda ps_, ex_, gl, nn: lambda e: e.activation(
            ex_[:, 0:gl * 512], ps_[:, 0:gl * 512], AF.Exp,
            bias=negm[:, nn:nn + 1], scale=0.125))(pss, exb, glen, n),
            waits=[t_sc, t_negm, prev_ctx])
        t_exp_t[gi] = t_exp
        t_m = t_exp
        for j in range(glen):
            kt = g0 + j
            dlt = kt * 128 - qb * 512
            if dlt >= 0:
                t_m = P.emit("dve", (lambda ex_, jj, dd: lambda e: e.tensor_tensor(
                    ex_[:, jj * 512:(jj + 1) * 512],
                    ex_[:, jj * 512:(jj + 1) * 512],
                    ut_sb[:, 384 - dd:896 - dd], ALU.mult))(exb, j, dlt),
                    waits=[t_exp, t_ut, t_m])
        t_mask_t[gi] = t_m

    def emit_ctx(gi):
        nonlocal t_psc_free, t_psbc_free, t_rcs_free, t_bcs_free, t_ctx_done
        nonlocal t_qtb_free, t_ktb_free
        (n, hb, rb, qb, g0, glen, lastq) = tasks[gi]
        bi = gi % 3
        exb = expb[bi]
        t_cm = None
        for j in range(glen):
            kt = g0 + j
            vsl = bass.AP(VBUF, kt * (NH * 65) + n * 65,
                          [[LT * NH * 65, 128], [1, 65]])
            w = [t_mask_t[gi], t_vb_done]
            if g0 == 0 and j == 0:
                w.append(t_psc_free)
            is_last = lastq and (j == glen - 1)
            t_cm = P.emit("pe", (lambda vs, ex_, jj, st, sp: lambda e: e.matmul(
                psc[:, :], vs, ex_[:, jj * 512:(jj + 1) * 512],
                start=st, stop=sp))(vsl, exb, j, g0 == 0 and j == 0, is_last),
                waits=w, n=(1 if is_last else 0))
        t_buf_prev[bi] = [t_exp_t[gi], t_cm]
        if lastq:
            t_rc = P.emit("dve", lambda e: e.reciprocal(
                rcs[64:65, :], psc[64:65, :]), waits=[t_cm, t_rcs_free])
            t_bc4 = P.emit("pe", lambda e: e.matmul(
                psbc[:, :], ones64[64:65, :], rcs[64:65, :],
                start=True, stop=True), waits=[t_rc, t_psbc_free, t_o64])
            t_rcs_free = t_bc4
            t_bce = P.emit("dve", lambda e: e.tensor_copy(
                bcs[:, :], psbc[:, :]), waits=[t_bc4, t_bcs_free])
            t_psbc_free = t_bce
            cdst = CTX[(n % 2) * 64:(n % 2) * 64 + 64,
                       (n // 2) * L + qb * 512:(n // 2) * L + (qb + 1) * 512]
            t_ce = P.emit("dve", (lambda cd: lambda e: e.tensor_tensor(
                cd, psc[0:64, :], bcs[:, :], ALU.mult))(cdst), waits=[t_bce])
            t_bcs_free = t_ce
            t_psc_free = t_ce
            t_ctx_done = t_ce
        if gi + 1 < NT and tasks[gi + 1][0] // 2 != n // 2:
            t_qtb_free[hb] = t_sc_t[gi]
            t_ktb_free[hb] = t_sc_t[gi]
        elif gi + 1 == NT:
            t_qtb_free[hb] = t_sc_t[gi]
            t_ktb_free[hb] = t_sc_t[gi]

    job_tick = 0
    for gi in range(NT + 2):
        if gi < NT:
            emit_scores_exp_mask(gi)
        if gi >= 2:
            emit_ctx(gi - 2)
        job_tick += 1
        if job_tick % 12 == 0:
            emit_tail(12)
    t_p4_pe = P.tok("pe")
    t_p4_act = P.tok("act")
    t_p4_dve = P.tok("dve")

    t_dbgc = None
    if dbg:
        t_dbgc = P.dma("sy", ctx_dbg.ap(), CTX[:, :], waits=[t_ctx_done])
    es4p.close()
    RB.cur = 0
    woT = [RB.take("woT0", [128, DT * 512], BF16), RB.take("woT1", [128, DT * 512], BF16)]
    stile = [RB.take("st0", [128, 512], BF16), RB.take("st1", [128, 512], BF16)]
    bias_sl5 = [RB.take("bias_sl0b", [1, 512], BF16), RB.take("bias_sl1b", [1, 512], BF16)]
    es5p = contextlib.ExitStack()
    ps5 = [PSUM("ps50", [128, 512], F32, es5p), PSUM("ps51", [128, 512], F32, es5p)]

    t_woT_free = [None, None]
    t_stile_free = [None, None]
    t_s_done = None
    t_ps5_free = [None, None]
    t_bsl5 = None
    for oc in range(OC):
        wb = oc % 2
        tws = []
        for dt in range(DT):
            tws.append(P.dma(
                "sy", woT[wb][:, dt * 512:(dt + 1) * 512],
                wbf["wo"].ap()[oc * 512:(oc + 1) * 512, dt * 128:(dt + 1) * 128],
                waits=[t_woT_free[wb], t_cast["wo"], t_p4_pe], transpose=True))
        bsl = oc % 2
        t_bsl5n = P.dma("sw", bias_sl5[bsl][0:1, :],
                        b_in["bo"].ap()[0:1, oc * 512:(oc + 1) * 512],
                        waits=[t_bsl5, t_p4_pe])
        t_last = None
        for lt in range(LT):
            pb5 = lt % 2
            ps = ps5[pb5]
            w = tws + [t_ps5_free[pb5], t_ctx_done, t_p4_act]
            for dt in range(DT):
                P.emit("pe", (lambda pp, dtt, ltt, wbb: lambda e: e.matmul(
                    pp[:, :], CTX[:, dtt * L + ltt * 128:dtt * L + ltt * 128 + 128],
                    woT[wbb][:, dtt * 512:(dtt + 1) * 512],
                    start=(dtt == 0), stop=False))(ps, dt, lt, wb),
                    waits=(w if dt == 0 else []), n=0)
            t_mm = P.emit("pe", (lambda pp, bb: lambda e: e.matmul(
                pp[:, :], ones_r[:, :], bias_sl5[bb][0:1, :],
                start=False, stop=True))(ps, bsl), waits=[t_ones, t_bsl5n])
            t_last = t_mm
            t_se = P.emit("dve", (lambda pp, sb2: lambda e: e.tensor_copy(
                sb2[:, :], pp[:, :]))(ps, stile[pb5]),
                waits=[t_mm, t_stile_free[pb5]])
            t_ps5_free[pb5] = t_se
            t_sw = P.dma("sq", s_sc.ap()[lt * 128:(lt + 1) * 128,
                                         oc * 512:(oc + 1) * 512],
                         stile[pb5][:, :], waits=[t_se])
            t_stile_free[pb5] = t_sw
            t_s_done = t_sw
            emit_tail(4)
        t_woT_free[wb] = t_last
        t_bsl5 = t_last
    t_s_done = P.qtok("sq")
    t_p5_pe = P.tok("pe")
    t_p5_dve = P.tok("dve")
    while (gj_i < len(gamma_jobs) or not all(p2m_done)
           or qj_i < len(quant_jobs) or rp_i < len(repack_jobs)):
        emit_tail(8)

    es5p.close()
    RA.reset()
    RB.cur = 0
    RC.reset()
    XNT2 = RA.take("XNT2", [128, DT * L], BF16)
    HIDT = RB.take("HIDT", [128, HT * 1024], BF16)
    w12b = [RC.take(f"w12b{i}", [128, DT * 256], BF16) for i in range(3)]
    LBS = 2 if L > 1024 else 1
    LBL = L // LBS
    t_xnt2 = []
    for dt in range(DT):
        t_xnt2.append(P.dma("sy", XNT2[:, dt * L:(dt + 1) * L],
                            xnT_sc.ap()[dt * 128:(dt + 1) * 128, :],
                            waits=t_xnt_w + [t_p5_pe]
                            + ([t_dbgc] if dbg else [])))
    sgb = [RC.take("sgb0", [128, 256], BF16), RC.take("sgb1", [128, 256], BF16)]
    x2b = [RC.take("x2b0", [128, 256], BF16), RC.take("x2b1", [128, 256], BF16)]
    RC.cur = 0
    w3b = [RC.take(f"w3b_{i}", [128, HT * 256], BF16) for i in range(2)]
    RC.cur = max(RC.cur, 3 * ((DT * 512 + 63) & ~63) + 2 * 512)
    gb6 = [RC.take("gb60", [128, 256], BF16), RC.take("gb61", [128, 256], BF16)]
    sob = [RC.take("sob0", [128, 256], BF16), RC.take("sob1", [128, 256], BF16)]
    xob = [RC.take("xob0", [128, 256], F32), RC.take("xob1", [128, 256], F32)]
    sgf6 = [RC.take("sgf60", [128, 256], F32), RC.take("sgf61", [128, 256], F32)]
    t_w3b_free = [None, None]
    t_p63_free = [None, None]
    pend_ost = [None]
    t_gb6_free = [None, None]
    t_xob_free = [None, None]
    t_sob_free = [None, None]
    t_sgf6_free = [None, None]
    t_hid_free = t_s_done + [t_p5_dve]
    t_wlast = None
    NHC = H // 256
    for lb in range(LBS):
        es6ap = contextlib.ExitStack()
        p61 = [PSUM(f"p61{lb}_{i}", [128, 512], F32, es6ap) for i in range(2)]
        p62 = [PSUM(f"p62{lb}_{i}", [128, 512], F32, es6ap) for i in range(2)]
        t_w12_free = [None, None, None]
        t_p61_free = [None, None]
        t_p62_free = [None, None]
        t_sgb_free = [None, None]
        t_x2b_free = [None, None]
        t_hid_done = None
        bi = 0
        for hc in range(NHC):
            if lb == 0 and (hc * 2 + 2) * 2048 > w3ch_rel:
                while w3_i < len(w3jobs):
                    emit_tail(8)
            b1 = bi % 3
            b2 = (bi + 1) % 3
            bi += 2
            in1 = bass.AP(w1qT, hc * (D * 256), [[256, 128], [128 * 256, DT], [1, 256]])
            in2 = bass.AP(w2qT, hc * (D * 256), [[256, 128], [128 * 256, DT], [1, 256]])
            tl1 = P.dma("sy", w12b[b1][:, :], in1,
                        waits=[t_w12_free[b1], t_rp_done[0], t_rp_done[1],
                               t_p4_pe, t_p4_dve, t_w3b_free[0],
                               t_w3b_free[1]])
            tl2 = P.dma("sy", w12b[b2][:, :], in2,
                        waits=[t_w12_free[b2], t_rp_done[0], t_rp_done[1],
                               t_p4_pe, t_p4_dve, t_w3b_free[0],
                               t_w3b_free[1]])
            t_l1 = None
            t_l2 = None
            for htl in range(2):
                ht = hc * 2 + htl
                for ls in range(LBL // 512):
                    lo = lb * LBL + ls * 512
                    lsp = ls % 2
                    pa = p61[lsp]
                    pc = p62[lsp]
                    for dt in range(DT):
                        P.emit("pe", (lambda pp, dtt, b_, ht_, lo_:
                                      lambda e: e.matmul(
                            pp[:, :],
                            w12b[b_][:, dtt * 256 + ht_ * 128:dtt * 256 + ht_ * 128 + 128],
                            XNT2[:, dtt * L + lo_:dtt * L + lo_ + 512],
                            start=(dtt == 0), stop=(dtt == DT - 1)))(pa, dt, b1, htl, lo),
                            waits=(([tl1, t_p61_free[lsp], t_p5_dve,
                                     t_p4_act] + t_xnt2)
                                   if dt == 0 else []),
                            n=(1 if dt == DT - 1 else 0))
                    t_l1 = P.tok("pe")
                    for dt in range(DT):
                        P.emit("pe", (lambda pp, dtt, b_, ht_, lo_:
                                      lambda e: e.matmul(
                            pp[:, :],
                            w12b[b_][:, dtt * 256 + ht_ * 128:dtt * 256 + ht_ * 128 + 128],
                            XNT2[:, dtt * L + lo_:dtt * L + lo_ + 512],
                            start=(dtt == 0), stop=(dtt == DT - 1)))(pc, dt, b2, htl, lo),
                            waits=([tl2, t_p62_free[lsp], t_p5_dve,
                                    t_p4_act] if dt == 0 else []),
                            n=(1 if dt == DT - 1 else 0))
                    t_l2 = P.tok("pe")
                    for sub in range(2):
                        t_sg = P.emit("act", (lambda pp, ss: lambda e: e.activation(
                            sgb[ss][:, :], pp[:, ss * 256:ss * 256 + 256], AF.Sigmoid,
                            scale=0.5))(pa, sub), waits=[t_l1, t_sgb_free[sub]])
                        t_x2 = P.emit("dve", (lambda pp, ss: lambda e: e.tensor_scalar_mul(
                            x2b[ss][:, :], pp[:, ss * 256:ss * 256 + 256], 0.5))(pc, sub),
                            waits=[t_l2, t_x2b_free[sub]])
                        t_hd = P.emit("dve", (lambda ht_, ls_, ss: lambda e: e.tensor_mul(
                            HIDT[:, ht_ * 1024 + ls_ * 512 + ss * 256:
                                 ht_ * 1024 + ls_ * 512 + ss * 256 + 256],
                            sgb[ss][:, :], x2b[ss][:, :]))(ht, ls, sub),
                            waits=[t_sg, t_x2, t_hid_free,
                                   (t_w3T_all[0]
                                    if (ht + 1) * 2048 > w3ch_rel
                                    else None)])
                        t_sgb_free[sub] = t_hd
                        t_x2b_free[sub] = t_hd
                        t_hid_done = t_hd
                    t_p61_free[lsp] = t_sgb_free[1]
                    t_p62_free[lsp] = t_x2b_free[1]
            t_w12_free[b1] = t_l1
            t_w12_free[b2] = t_l2
            emit_tail(4)
        while w3_i < len(w3jobs):
            emit_tail(8)
        es6ap.close()
        es6bp = contextlib.ExitStack()
        p63 = [PSUM(f"p63{lb}_{i}", [128, 2048], F32, es6bp) for i in range(2)]
        NLB = LBL // 128
        HT2 = HT // 2
        for ob2 in range(D // 512):
            tl_h = [None, None]
            for h in range(2):
                in3 = bass.AP(w3qT, ob2 * (H * 512) + h * HT2 * 128 * 512,
                              [[512, 128], [128 * 512, HT2], [1, 512]])
                tl_h[h] = P.dma("sy", w3b[h][:, :], in3,
                                waits=[t_w3b_free[h], t_w3T_all[0], t_hid_done,
                                       t_w12_free[0], t_w12_free[1],
                                       t_w12_free[2]])
            for ltq in range(NLB // 4):
                pq = ltq % 2
                ps = p63[pq]
                for h in range(2):
                    for htl in range(HT2):
                        ht = h * HT2 + htl
                        for lt4 in range(4):
                            ltl = ltq * 4 + lt4
                            P.emit("pe", (lambda pp, ht_, ltl_, l4, hh, hl:
                                          lambda e: e.matmul(
                                pp[:, l4 * 512:(l4 + 1) * 512],
                                HIDT[:, ht_ * 1024 + ltl_ * 128:
                                     ht_ * 1024 + ltl_ * 128 + 128],
                                w3b[hh][:, hl * 512:(hl + 1) * 512],
                                start=(ht_ == 0), stop=(ht_ == HT - 1)))(
                                    ps, ht, ltl, lt4, h, htl),
                                waits=(([tl_h[h], t_p63_free[pq]])
                                       if htl == 0 and lt4 == 0 else []),
                                n=(1 if ht == HT - 1 and lt4 == 3 else 0))
                t_wlast = P.tok("pe")
                if ltq == NLB // 4 - 1:
                    t_w3b_free[0] = t_wlast
                    t_w3b_free[1] = t_wlast
                t_g = None
                for lt4 in range(4):
                    ltl = ltq * 4 + lt4
                    lo = lb * LBL + ltl * 128
                    for ss in range(2):
                        ob = ob2 * 2 + ss
                        xb_ = (ltl * 2 + ss) % 2
                        t_xl = P.dma("sy", xob[xb_][:, :],
                                     x_in.ap()[lo:lo + 128,
                                               ob * 256:(ob + 1) * 256],
                                     waits=[t_xob_free[xb_], t_p4_pe])
                        t_sl = P.dma("sw", sob[xb_][:, :],
                                     s_sc.ap()[lo:lo + 128,
                                               ob * 256:(ob + 1) * 256],
                                     waits=[t_sob_free[xb_], t_s_done,
                                            t_p4_pe])
                        if pend_ost[0] is not None:
                            po_, pou, pxb = pend_ost[0]
                            t_od = P.dma("sw", po_, xob[pxb][:, :], waits=[pou])
                            t_xob_free[pxb] = t_od
                            pend_ost[0] = None
                        t_g = P.emit("act", (lambda pp, l4, s_, g_: lambda e:
                                             e.activation(
                            gb6[g_][:, :], pp[:, l4 * 512 + s_ * 256:
                                          l4 * 512 + s_ * 256 + 256], AF.Silu,
                            scale=0.5))(ps, lt4, ss, xb_),
                            waits=[t_wlast, t_gb6_free[xb_]])
                        t_sg6 = P.emit("dve", (lambda xb2: lambda e: e.tensor_mul(
                            sgf6[xb2][:, :], gb6[xb2][:, :], sob[xb2][:, :]))(xb_),
                            waits=[t_g, t_sl, t_sgf6_free[xb_]])
                        t_gb6_free[xb_] = t_sg6
                        t_sob_free[xb_] = t_sg6
                        t_ou = P.emit("dve", (lambda xb2: lambda e: e.tensor_add(
                            xob[xb2][:, :], sgf6[xb2][:, :], xob[xb2][:, :]))(xb_),
                            waits=[t_sg6, t_xl])
                        t_sgf6_free[xb_] = t_ou
                        pend_ost[0] = (out_t.ap()[lo:lo + 128,
                                                  ob * 256:(ob + 1) * 256],
                                       t_ou, xb_)
                t_p63_free[pq] = t_g
        t_hid_free = t_wlast
        es6bp.close()
    if pend_ost[0] is not None:
        po_, pou, pxb = pend_ost[0]
        t_od = P.dma("sw", po_, xob[pxb][:, :], waits=[pou])
        t_xob_free[pxb] = t_od
        pend_ost[0] = None
    P.emit("po", lambda e: e.nop(),
           waits=P.qtok("sw") + P.qtok("sq") + P.qtok("sy"))

    with nc.allow_low_precision(reason="bf16 transposes and bf16 staging by design"), \
         nc.Block() as block:
        @block.sync
        def _(e):
            P.replay("sy", e)

        @block.scalar
        def _(e):
            P.replay("act", e)

        @block.tensor
        def _(e):
            P.replay("pe", e)

        @block.vector
        def _(e):
            P.replay("dve", e)

        @block.gpsimd
        def _(e):
            P.replay("po", e)

    return nc


def make_consts(cfg: Cfg):
    L, NH, LT = cfg.L, cfg.NH, cfg.LT
    pos = np.arange(L, dtype=np.float64)
    ts = 10000.0 ** (np.arange(NH, dtype=np.float64) / 32.0)
    ang = pos[:, None] / ts[None, :]
    cos = np.cos(ang).astype(np.float32)
    sin = np.sin(ang).astype(np.float32)
    cos_c = np.ascontiguousarray(
        cos.reshape(LT, 128, NH).transpose(1, 0, 2).reshape(128, LT * NH)
    ).astype(ml_dtypes.bfloat16)
    sin_c = np.ascontiguousarray(
        sin.reshape(LT, 128, NH).transpose(1, 0, 2).reshape(128, LT * NH)
    ).astype(ml_dtypes.bfloat16)
    ut = (np.arange(896)[None, :] >= 384 + np.arange(128)[:, None]).astype(ml_dtypes.bfloat16)
    ident = np.eye(128, dtype=ml_dtypes.bfloat16)
    return cos_c, sin_c, ut, ident


def make_in_maps(cfg, x_all, wq_, bq_, wk_, bk_, wv_, bv_, wo_, bo_, w1_, w2_, w3_,
                 n_cores):
    cos_c, sin_c, ut, ident = make_consts(cfg)
    common = {
        "wq": np.ascontiguousarray(wq_, np.float32),
        "wk": np.ascontiguousarray(wk_, np.float32),
        "wv": np.ascontiguousarray(wv_, np.float32),
        "wo": np.ascontiguousarray(wo_, np.float32),
        "bq": np.ascontiguousarray(bq_, np.float32).reshape(1, -1),
        "bk": np.ascontiguousarray(bk_, np.float32).reshape(1, -1),
        "bv": np.ascontiguousarray(bv_, np.float32).reshape(1, -1),
        "bo": np.ascontiguousarray(bo_, np.float32).reshape(1, -1),
        "w1": np.ascontiguousarray(w1_, np.float32),
        "w2": np.ascontiguousarray(w2_, np.float32),
        "w3": np.ascontiguousarray(w3_, np.float32),
        "cos_c": cos_c, "sin_c": sin_c, "ut_c": ut, "id_c": ident,
    }
    xs = np.ascontiguousarray(np.asarray(x_all, np.float32))
    in_maps = []
    for b in range(n_cores):
        m = dict(common)
        m["x"] = np.ascontiguousarray(xs[b])
        in_maps.append(m)
    return in_maps


def kernel(L0_attn_x, L0_attn_wq_w, L0_attn_wq_b, L0_attn_wk_w, L0_attn_wk_b,
           L0_attn_wv_w, L0_attn_wv_b, L0_attn_wo_w, L0_attn_wo_b,
           L0_ffn_w1, L0_ffn_w2, L0_ffn_w3):
    cfg = Cfg()
    nc = build_nc(cfg)
    in_maps = make_in_maps(cfg, L0_attn_x, L0_attn_wq_w, L0_attn_wq_b,
                           L0_attn_wk_w, L0_attn_wk_b, L0_attn_wv_w, L0_attn_wv_b,
                           L0_attn_wo_w, L0_attn_wo_b, L0_ffn_w1, L0_ffn_w2,
                           L0_ffn_w3, 8)
    res = run_bass_kernel_spmd(nc, in_maps, core_ids=list(range(8)))
    return np.stack([np.asarray(res.results[b]["out"], np.float32) for b in range(8)])


# version 27
# speedup vs baseline: 1.0468x; 1.0468x over previous
import contextlib
import numpy as np
import ml_dtypes
import concourse.bass as bass
import concourse.mybir as mybir
from concourse.bass_utils import run_bass_kernel_spmd

F32 = mybir.dt.float32
BF16 = mybir.dt.bfloat16
AF = mybir.ActivationFunctionType
ALU = mybir.AluOpType
AX = mybir.AxisListType
ESZ = {F32: 4, BF16: 2}


class Cfg:
    def __init__(self, L=2048, NH=32, H=5632):
        self.L = L
        self.NH = NH
        self.HD = 64
        self.D = NH * self.HD
        self.H = H
        self.LT = L // 128
        self.DT = self.D // 128
        self.HT = H // 128
        self.OC = self.D // 512
        self.QB = L // 512


STREAM_OF_Q = {"sy": "sy", "sq": "act", "sw": "po"}
ENG_SEMS = ("pe", "act", "dve", "po")
ALL_STREAMS = ("sy", "act", "pe", "dve", "po")
POOL_N = {"sy": 48, "sw": 32, "sq": 8}


class Prog:

    def __init__(self, nc, es):
        self.nc = nc
        self._es = es
        self.items = {s: [] for s in ALL_STREAMS}
        self.sems = {}
        self.cnt = {}
        self.pending = {s: [] for s in ALL_STREAMS}
        self.touched = set()
        for s in ENG_SEMS:
            self.sems[s] = es.enter_context(nc.semaphore("S_" + s))
            self.cnt[s] = 0
        self.pool = {}
        self.pidx = {}
        for q, npool in POOL_N.items():
            self.pool[q] = []
            self.pidx[q] = 0
            for i in range(npool):
                name = f"{q}{i}"
                self.sems[name] = es.enter_context(nc.semaphore("S_" + name))
                self.cnt[name] = 0
                self.pool[q].append(name)

    def tok(self, sem):
        return (sem, self.cnt[sem])

    def emit(self, sem, fn, waits=(), n=1, stream=None):
        if stream is None:
            stream = sem
        ws = []
        for w in waits:
            if w is None:
                continue
            if isinstance(w, list):
                ws.extend(x for x in w if x is not None and x[1] > 0)
            elif w[1] > 0:
                ws.append(w)
        if self.pending[stream]:
            ws = self.pending[stream] + ws
            self.pending[stream] = []
        self.cnt[sem] += n
        self.touched.add(sem)
        self.items[stream].append((ws, fn, self.sems[sem], n))
        return (sem, self.cnt[sem])

    def dma_once(self, q, out, in_, waits=(), transpose=False):
        name = f"once{len([k for k in self.sems if k.startswith('once')])}"
        self.sems[name] = self._es.enter_context(self.nc.semaphore("S_" + name))
        self.cnt[name] = 0
        if transpose:
            fn = lambda e: e.dma_start(out=out, in_=in_, transpose=True)
        else:
            fn = lambda e: e.dma_start(out=out, in_=in_)
        return self.emit(name, fn, waits, n=16, stream=STREAM_OF_Q[q])

    def dma(self, q, out, in_, waits=(), transpose=False):
        assert q in ("sy", "sq", "sw")
        sem = self.pool[q][self.pidx[q]]
        self.pidx[q] = (self.pidx[q] + 1) % len(self.pool[q])
        if transpose:
            fn = lambda e: e.dma_start(out=out, in_=in_, transpose=True)
        else:
            fn = lambda e: e.dma_start(out=out, in_=in_)
        return self.emit(sem, fn, waits, n=16, stream=STREAM_OF_Q[q])

    def qtok(self, q):
        return [(s, self.cnt[s]) for s in self.pool[q] if self.cnt[s] > 0]

    def barrier(self):
        toks = [(s, self.cnt[s]) for s in self.touched if self.cnt[s] > 0]
        self.touched = set()
        for s in ALL_STREAMS:
            self.pending[s] = self.pending[s] + toks

    def replay(self, stream, eng):
        hi = {}
        for ws, fn, sem, n in self.items[stream]:
            for (wsem, wcnt) in ws:
                if wcnt > hi.get(wsem, 0):
                    eng.wait_ge(self.sems[wsem], wcnt)
                    hi[wsem] = wcnt
            ins = fn(eng)
            if n > 0:
                ins.then_inc(sem, n)


class Region:
    def __init__(self, nc, base, size):
        self.nc = nc
        self.base = base
        self.size = size
        self.cur = 0

    def take(self, name, sh, dt):
        nb = sh[1] * ESZ[dt]
        nb = (nb + 63) & ~63
        assert self.cur + nb <= self.size, f"{name}: region overflow {self.cur + nb} > {self.size}"
        t = self.nc.alloc_sbuf_tensor_at(name, sh, dt, offset=self.base + self.cur)
        self.cur += nb
        return t

    def reset(self):
        self.cur = 0


def build_nc(cfg: Cfg, dbg=False):
    L, D, H, NH, HD = cfg.L, cfg.D, cfg.H, cfg.NH, cfg.HD
    LT, DT, HT, OC, QB = cfg.LT, cfg.DT, cfg.HT, cfg.OC, cfg.QB
    nc = bass.Bass("TRN2", target_bir_lowering=False, detect_race_conditions=False)

    x_in = nc.dram_tensor("x", [L, D], F32, kind="ExternalInput")
    w_in = {m: nc.dram_tensor(m, [D, D], F32, kind="ExternalInput")
            for m in ("wq", "wk", "wv", "wo")}
    b_in = {m: nc.dram_tensor(m, [1, D], F32, kind="ExternalInput")
            for m in ("bq", "bk", "bv", "bo")}
    w1 = nc.dram_tensor("w1", [H, D], F32, kind="ExternalInput")
    w2 = nc.dram_tensor("w2", [H, D], F32, kind="ExternalInput")
    w3 = nc.dram_tensor("w3", [D, H], F32, kind="ExternalInput")
    cos_c = nc.dram_tensor("cos_c", [128, LT * NH], BF16, kind="ExternalInput")
    sin_c = nc.dram_tensor("sin_c", [128, LT * NH], BF16, kind="ExternalInput")
    ut_c = nc.dram_tensor("ut_c", [128, 896], BF16, kind="ExternalInput")
    id_c = nc.dram_tensor("id_c", [128, 128], BF16, kind="ExternalInput")
    out_t = nc.dram_tensor("out", [L, D], F32, kind="ExternalOutput")

    KI = "ExternalOutput" if dbg else "Internal"
    wbf = {m: nc.dram_tensor(f"{m}_bf", [D, D], BF16, kind=KI)
           for m in ("wq", "wk", "wv", "wo")}
    xnT_sc = nc.dram_tensor("xnT_sc", [D, L], BF16, kind=KI)
    q_sc = nc.dram_tensor("q_sc", [L, D], BF16, kind=KI)
    k_sc = nc.dram_tensor("k_sc", [L, D], BF16, kind=KI)
    s_sc = nc.dram_tensor("s_sc", [L, D], BF16, kind=KI)
    w1q = nc.dram_tensor("w1q", [H, D], BF16, kind=KI)
    w2q = nc.dram_tensor("w2q", [H, D], BF16, kind=KI)
    w3q = nc.dram_tensor("w3q", [D, H], BF16, kind=KI)
    w1qT = nc.dram_tensor("w1qT", [H // 256, D * 256], BF16, kind=KI)
    w2qT = nc.dram_tensor("w2qT", [H // 256, D * 256], BF16, kind=KI)
    w3qT = nc.dram_tensor("w3qT", [D // 512, H * 512], BF16, kind=KI)
    ctx_dbg = nc.dram_tensor("ctx_dbg", [128, DT * L], BF16, kind=KI) if dbg else None

    es = contextlib.ExitStack()
    P = Prog(nc, es)

    def SB(n, sh, dt, st=es):
        return st.enter_context(nc.sbuf_tensor(n, sh, dt))

    def PSUM(n, sh, dt, st):
        return st.enter_context(nc.psum_tensor(n, sh, dt))

    TOP = nc.sbuf_top
    A_SZ = DT * L * 2
    B_SZ = max(HT * 1024 * 2, ((LT * NH * 65 * 2 + 63) & ~63) + 22528, 16 * D + 8 * H)
    P4RC = H * 2 + 1024 + 1024 + 1792 + 6 * L * 2
    C_SZ = max(max(2 * DT * 1024 + 2 * (LT * NH * 2) + 2048, P4RC)
               + 8192 + 4096 + 512,
               3 * DT * 512 + 2048,
               2 * HT * 512 + 7680)
    A_OFF = TOP - A_SZ
    B_OFF = A_OFF - B_SZ
    C_OFF = B_OFF - C_SZ
    nc.sbuf_top = C_OFF
    RA = Region(nc, A_OFF, A_SZ)
    RB = Region(nc, B_OFF, B_SZ)
    RC = Region(nc, C_OFF, C_SZ)

    id_sb = SB("id_sb", [128, 128], BF16)
    ones_r = SB("ones_r", [1, 128], BF16)
    ones_rf = SB("ones_rf", [1, 128], F32)
    ones_cf = SB("ones_cf", [128, 1], F32)
    GWc = min(1024, D)
    NGJ = 2 * (H // 128) * ((D + GWc - 1) // GWc) + (D // 128) * ((H + GWc - 1) // GWc)
    gacc = SB("gacc", [128, NGJ + 4], F32)
    th_t = SB("th_t", [1, 4], F32)
    thv = SB("thv", [128, 3], F32)
    nthv = SB("nthv", [128, 3], F32)
    qn2m = SB("qn2m", [128, NH], F32)
    kn2m = SB("kn2m", [128, NH], F32)
    negm = SB("negm", [128, NH], F32)
    nrm8 = SB("nrm8", [128, 8], F32)
    stat = SB("stat", [128, 20], F32)
    ones64 = SB("ones64", [128, 64], BF16)

    t_id = P.dma("sy", id_sb[:, :], id_c.ap())
    t_cast = {}
    for m in ("wq", "wk", "wv", "wo"):
        t_cast[m] = P.dma_once("sw", wbf[m].ap(), w_in[m].ap())
    t_ones = P.emit("dve", lambda e: e.memset(ones_r[:, :], 1.0))
    t_ones2 = P.emit("dve", lambda e: e.memset(ones_rf[:, :], 1.0))
    t_ones3 = P.emit("dve", lambda e: e.memset(ones_cf[:, :], 1.0))
    t_o64 = P.emit("dve", lambda e: e.memset(ones64[64:65, :], 1.0))
    t_qn0 = P.emit("dve", lambda e: e.memset(qn2m[:, :], 0.0))
    t_kn0 = P.emit("dve", lambda e: e.memset(kn2m[:, :], 0.0))

    XNT = RA.take("XNT", [128, DT * L], BF16)
    xa = [RB.take("xa0", [128, D], F32), RB.take("xa1", [128, D], F32)]
    sqb = RB.take("sqb", [128, D], F32)
    xcb = [RB.take("xcb0", [128, D], BF16), RB.take("xcb1", [128, D], BF16)]
    p1_end = RB.cur
    GW = min(1024, D)
    gch = [nc.alloc_sbuf_tensor_at("gch0", [128, GW], F32,
                                   offset=B_OFF + B_SZ - 2 * GW * 4),
           nc.alloc_sbuf_tensor_at("gch1", [128, GW], F32,
                                   offset=B_OFF + B_SZ - GW * 4)]
    es1p = contextlib.ExitStack()
    pstr = [PSUM("pstr0", [128, 1024], BF16, es1p),
            PSUM("pstr1", [128, 1024], BF16, es1p)]

    gamma_jobs = []
    gbnd = []
    for mi, (wsrc, rows, width) in enumerate(((w1, H, D), (w2, H, D), (w3, D, H))):
        for r0 in range(0, rows, 128):
            for c0 in range(0, width, GW):
                cw = min(GW, width - c0)
                gamma_jobs.append((wsrc.ap()[r0:r0 + 128, c0:c0 + cw], cw))
        gbnd.append(len(gamma_jobs))
    gj_i = 0
    t_gred = [None, None]
    t_gam_last = None

    def emit_gamma_some(k):
        nonlocal gj_i, t_gam_last
        for _ in range(k):
            if gj_i >= len(gamma_jobs):
                return
            src, cw = gamma_jobs[gj_i]
            b = gj_i % 2
            tl = P.dma("sq", gch[b][:, 0:cw], src, waits=[t_gred[b]])
            t_gred[b] = P.emit("dve", (lambda bb, cc, ww: lambda e: e.tensor_reduce(
                gacc[:, cc:cc + 1], gch[bb][:, 0:ww], AX.X, ALU.add,
                apply_absolute_value=True))(b, gj_i, cw), waits=[tl])
            t_gam_last = t_gred[b]
            gj_i += 1

    NHALF = max(1, DT // 8)
    t_xa_free = [None, None]
    t_xcb_free = [None, None]
    t_sqb_free = None
    t_pstr_free = [None, None]
    t_xnt_w = [None] * LT
    t_xnt_ready = [None] * LT
    t_ln_last = []
    invD = 1.0 / D

    def emit_ln(lt):
        nonlocal t_sqb_free
        b = lt % 2
        sb_ = b * 10
        t_x = P.dma("sy", xa[b][:, :], x_in.ap()[lt * 128:(lt + 1) * 128, :],
                    waits=[t_xa_free[b]])
        t_sq = P.emit("act", (lambda bb: lambda e: e.activation(
            sqb[:, :], xa[bb][:, :], AF.Square))(b), waits=[t_x, t_sqb_free])
        t_s1 = P.emit("dve", (lambda bb, c: lambda e: e.tensor_reduce(
            stat[:, c:c + 1], xa[bb][:, :], AX.X, ALU.add))(b, sb_), waits=[t_x])
        t_s2 = P.emit("dve", (lambda c: lambda e: e.tensor_reduce(
            stat[:, c + 1:c + 2], sqb[:, :], AX.X, ALU.add))(sb_), waits=[t_sq])
        t_sqb_free = t_s2
        t_mu = P.emit("dve", (lambda c: lambda e: e.tensor_scalar_mul(
            stat[:, c + 2:c + 3], stat[:, c:c + 1], invD))(sb_), waits=[t_s1])
        t_m2 = P.emit("dve", (lambda c: lambda e: e.tensor_scalar_mul(
            stat[:, c + 3:c + 4], stat[:, c + 1:c + 2], invD))(sb_), waits=[t_s2])
        t_mm2 = P.emit("dve", (lambda c: lambda e: e.tensor_mul(
            stat[:, c + 4:c + 5], stat[:, c + 2:c + 3], stat[:, c + 2:c + 3]))(sb_),
            waits=[t_mu])
        t_var = P.emit("dve", (lambda c: lambda e: e.tensor_sub(
            stat[:, c + 5:c + 6], stat[:, c + 3:c + 4], stat[:, c + 4:c + 5]))(sb_),
            waits=[t_m2, t_mm2])
        t_ve = P.emit("dve", (lambda c: lambda e: e.tensor_scalar_add(
            stat[:, c + 6:c + 7], stat[:, c + 5:c + 6], 1e-5))(sb_), waits=[t_var])
        t_sd = P.emit("act", (lambda c: lambda e: e.activation(
            stat[:, c + 7:c + 8], stat[:, c + 6:c + 7], AF.Sqrt))(sb_), waits=[t_ve])
        t_rs = P.emit("dve", (lambda c: lambda e: e.reciprocal(
            stat[:, c + 8:c + 9], stat[:, c + 7:c + 8]))(sb_), waits=[t_sd])
        t_xc = P.emit("dve", (lambda bb, c: lambda e: e.tensor_scalar(
            xcb[bb][:, :], xa[bb][:, :], stat[:, c + 2:c + 3], stat[:, c + 8:c + 9],
            ALU.subtract, ALU.mult))(b, sb_), waits=[t_rs, t_xcb_free[b]])
        t_xa_free[b] = [t_xc, t_sq]
        t_tr_last = None
        t_ev = None
        for half in range(NHALF):
            nsub = min(8, DT - half * 8)
            hb = half % 2
            pb = pstr[hb]
            tm = None
            for j in range(nsub):
                dt = half * 8 + j
                tm = P.emit("pe", (lambda bb, dtt, jj, pbb: lambda e: e.transpose(
                    pbb[:, jj * 128:(jj + 1) * 128],
                    xcb[bb][:, dtt * 128:(dtt + 1) * 128], id_sb[:, :]))(b, dt, j, pb),
                    waits=[t_xc, t_id, t_pstr_free[hb]])
            t_tr_last = tm
            xv = bass.AP(XNT, (half * 8) * L + lt * 128,
                         [[DT * L, 128], [L, nsub], [1, 128]])
            pv = bass.AP(pb, 0, [[1024, 128], [128, nsub], [1, 128]])
            t_ev = P.emit("dve", (lambda xvv, pvv: lambda e: e.tensor_copy(xvv, pvv))(
                xv, pv), waits=[tm])
            t_pstr_free[hb] = t_ev
        t_xcb_free[b] = t_tr_last
        xsrc = bass.AP(XNT, lt * 128, [[DT * L, 128], [L, DT], [1, 128]])
        xdst = bass.AP(xnT_sc, lt * 128, [[L, 128], [128 * L, DT], [1, 128]])
        t_xnt_w[lt] = P.dma("sw", xdst, xsrc, waits=[t_ev])
        t_xnt_ready[lt] = P.tok("dve")
        t_ln_last[:] = [t_xc, t_sq]
        emit_gamma_some(14)

    es2p = contextlib.ExitStack()
    t_thv = []

    def emit_p2():
        pg = PSUM("pg", [1, 1], F32, es2p)
        pb2 = PSUM("pb2", [128, 1], F32, es2p)
        t_pg_free = None
        t_pb2_free = None
        t_csl = None
        for mi, (rows, width) in enumerate(((H, D), (H, D), (D, H))):
            c0 = 0 if mi == 0 else gbnd[mi - 1]
            c1 = gbnd[mi]
            t_csl = P.emit("dve", (lambda a0, a1, m: lambda e: e.tensor_reduce(
                gacc[:, NGJ + m:NGJ + m + 1], gacc[:, a0:a1], AX.X, ALU.add))(
                    c0, c1, mi), waits=[t_gam_last, t_csl])
            t_pg = P.emit("pe", (lambda m: lambda e: e.matmul(
                pg[:, :], gacc[:, NGJ + m:NGJ + m + 1], ones_cf[:, :],
                start=True, stop=True))(mi),
                waits=[t_csl, t_ones3, t_pg_free])
            t_th = P.emit("dve", (lambda m, ne: lambda e: e.tensor_scalar(
                th_t[0:1, m:m + 1], pg[:, :], 0.5 / ne, 0.5e-8, ALU.mult, ALU.add))(
                    mi, rows * width), waits=[t_pg])
            t_pg_free = t_th
            t_pb = P.emit("pe", (lambda m: lambda e: e.matmul(
                pb2[:, :], ones_rf[:, :], th_t[0:1, m:m + 1],
                start=True, stop=True))(mi), waits=[t_th, t_ones2, t_pb2_free])
            t_tv = P.emit("dve", (lambda m: lambda e: e.tensor_copy(
                thv[:, m:m + 1], pb2[:, :]))(mi), waits=[t_pb])
            t_pb2_free = t_tv
            t_nv = P.emit("dve", (lambda m: lambda e: e.tensor_scalar_mul(
                nthv[:, m:m + 1], thv[:, m:m + 1], -1.0))(mi), waits=[t_tv])
            t_thv.append((t_tv, t_nv))
        es2p.close()

    RB.reset()
    VBUF = RB.take("VBUF", [128, LT * NH * 65], BF16)
    RB.cur = max(RB.cur, p1_end)
    vb_save = RB.cur
    qe = [RB.take("qe0", [128, 512], BF16), RB.take("qe1", [128, 512], BF16)]
    rtmp = RB.take("rtmp", [128, 512], BF16)
    qr = [RB.take("qr0", [128, 512], BF16), RB.take("qr1", [128, 512], BF16)]
    sqf = RB.take("sqf", [128, 512], BF16)
    rtmp2 = RB.take("rtmp2", [128, 512], BF16)
    RC.reset()
    wXT = [RC.take("wXT0", [128, DT * 512], BF16), RC.take("wXT1", [128, DT * 512], BF16)]
    cos_sb = RC.take("cos_sb", [128, LT * NH], BF16)
    sin_sb = RC.take("sin_sb", [128, LT * NH], BF16)
    bias_sl = [RC.take("bias_sl0", [1, 512], BF16), RC.take("bias_sl1", [1, 512], BF16)]
    RC.cur = max(RC.cur, P4RC)
    dch = [RC.take("dch0", [128, 1024], F32), RC.take("dch1", [128, 1024], F32)]
    q2a = RC.take("q2a", [128, 1024], BF16)
    q2cb = RC.take("q2c", [128, 1024], BF16)
    t_cos = P.dma("sy", cos_sb[:, :], cos_c.ap())
    t_sin = P.dma("sy", sin_sb[:, :], sin_c.ap())
    es3p = contextlib.ExitStack()
    pqkv = [PSUM(f"pqkv{i}", [128, 512], F32, es3p) for i in range(3)]

    QW = min(1024, D)
    quant_jobs = []
    for mi, (wsrc, wdst, rows) in enumerate(((w1, w1q, H), (w2, w2q, H))):
        for r0 in range(0, rows, 128):
            for c0 in range(0, D, QW):
                quant_jobs.append((mi, wsrc.ap()[r0:r0 + 128, c0:c0 + QW],
                                   wdst.ap()[r0:r0 + 128, c0:c0 + QW]))
    qj_i = 0
    ql_i = 0
    t_dch_free = [None, None]
    t_dch_load = [None, None]
    t_q2c_free = None
    t_q2a_free = None
    t_qdone = {0: None, 1: None}

    def emit_quant_some(k):
        nonlocal qj_i, ql_i, t_q2c_free, t_q2a_free
        for _ in range(k):
            if qj_i >= len(quant_jobs):
                return
            while ql_i < min(qj_i + 2, len(quant_jobs)):
                mi_, src_, dst_ = quant_jobs[ql_i]
                db_ = ql_i % 2
                t_dch_load[db_] = P.dma("sq", dch[db_][:, 0:QW], src_,
                                        waits=[t_dch_free[db_]])
                ql_i += 1
            mi, src, dst = quant_jobs[qj_i]
            db = qj_i % 2
            tl = t_dch_load[db]
            ts1 = P.emit("act", (lambda m, d_: lambda e: e.activation(
                q2a[:, 0:QW], dch[d_][:, 0:QW], AF.Sign, bias=nthv[:, m:m + 1]))(
                    mi, db), waits=[tl, t_thv[mi][1], t_q2a_free])
            tb = P.emit("dve", (lambda m, d_: lambda e: e.tensor_scalar(
                q2cb[:, 0:QW], dch[d_][:, 0:QW], nthv[:, m:m + 1], 2.0,
                ALU.is_ge, ALU.mult))(mi, db),
                waits=[tl, t_thv[mi][1], t_q2c_free])
            t_dch_free[db] = [ts1, tb]
            tq = P.emit("dve", lambda e: e.scalar_tensor_tensor(
                q2cb[:, 0:QW], q2cb[:, 0:QW], -1.0, q2a[:, 0:QW],
                ALU.add, ALU.add), waits=[ts1, tb])
            t_q2a_free = tq
            tw = P.dma("sw", dst, q2cb[:, 0:QW], waits=[tq])
            t_q2c_free = tw
            if qj_i + 1 >= len(quant_jobs) or quant_jobs[qj_i + 1][0] != mi:
                t_qdone[mi] = P.qtok("sw")
            qj_i += 1

    p2_done = [False]
    t_wXT_free = [None, None]
    t_qe_free = [None, None]
    t_rtmp_free = [None, None]
    t_rtmp2_free = [None, None]
    t_qr_free = [None, None]
    t_sqf_free = None
    t_nrm8_free = None
    t_vb_done = None
    t_norm_acc = {"q": None, "k": None}
    t_ps_free = [None, None, None]
    t_bias = [None, None]
    bias_src = {"q": b_in["bq"], "k": b_in["bk"], "v": b_in["bv"]}
    psi = 0
    bsl_i = 0
    for mat in ("q", "k", "v"):
        for oc in range(OC):
            wb = ("qkv".index(mat) * OC + oc) % 2
            tws = []
            for dt in range(DT):
                tws.append(P.dma(
                    "sy", wXT[wb][:, dt * 512:(dt + 1) * 512],
                    wbf["w" + mat].ap()[oc * 512:(oc + 1) * 512, dt * 128:(dt + 1) * 128],
                    waits=[t_wXT_free[wb], t_cast["w" + mat]]
                    + (t_ln_last if mat == "v" and oc == 0 else []),
                    transpose=True))
            t_w_all = list(tws)
            bsl = bsl_i % 2
            bsl_i += 1
            t_bsl = P.dma("sw", bias_sl[bsl][0:1, :],
                          bias_src[mat].ap()[0:1, oc * 512:(oc + 1) * 512],
                          waits=[t_bias[bsl]])
            gi0 = "qkv".index(mat) * OC + oc
            t_last_mm = None
            for lt in range(LT):
                if gi0 == 0:
                    emit_ln(lt)
                myps = psi % 3
                ps = pqkv[myps]
                psi += 1
                mm_waits = t_w_all + [t_ps_free[myps], t_xnt_ready[lt]]
                for dt in range(DT):
                    P.emit("pe", (lambda pp, dtt, ltt, wbb: lambda e: e.matmul(
                        pp[:, :], XNT[:, dtt * L + ltt * 128:dtt * L + ltt * 128 + 128],
                        wXT[wbb][:, dtt * 512:(dtt + 1) * 512],
                        start=(dtt == 0), stop=False))(ps, dt, lt, wb),
                        waits=(mm_waits if dt == 0 else []), n=0)
                t_mm = P.emit("pe", (lambda pp, bb: lambda e: e.matmul(
                    pp[:, :], ones_r[:, :], bias_sl[bb][0:1, :],
                    start=False, stop=True))(ps, bsl), waits=[t_ones, t_bsl])
                t_last_mm = t_mm
                if mat in ("q", "k"):
                    qb_ = lt % 2
                    myqe = qe[lt % 2]
                    t_qev = P.emit("act", (lambda pp, qq: lambda e: e.copy(
                        qq[:, :], pp[:, :]))(ps, myqe),
                        waits=[t_mm, t_qe_free[lt % 2]])
                    t_ps_free[myps] = t_qev
                    co = bass.AP(cos_sb, lt * NH + oc * 8,
                                 [[LT * NH, 128], [1, 8], [0, 32]])
                    si = bass.AP(sin_sb, lt * NH + oc * 8,
                                 [[LT * NH, 128], [1, 8], [0, 32]])
                    qeA = bass.AP(myqe, 0, [[512, 128], [64, 8], [1, 32]])
                    qeB = bass.AP(myqe, 32, [[512, 128], [64, 8], [1, 32]])
                    tA = bass.AP(rtmp, 0, [[512, 128], [32, 8], [1, 32]])
                    tB = bass.AP(rtmp, 256, [[512, 128], [32, 8], [1, 32]])
                    qrA = bass.AP(qr[qb_], 0, [[512, 128], [64, 8], [1, 32]])
                    qrB = bass.AP(qr[qb_], 32, [[512, 128], [64, 8], [1, 32]])
                    tC = bass.AP(rtmp2, 0, [[512, 128], [32, 8], [1, 32]])
                    tD = bass.AP(rtmp2, 256, [[512, 128], [32, 8], [1, 32]])
                    TT = lambda a1, a2, a3, op: (lambda e: e.tensor_tensor(a1, a2, a3, op))
                    w1_ = P.emit("dve", TT(tA, qeA, co, ALU.mult),
                                 waits=[t_qev, t_cos, t_rtmp_free[0]])
                    w2_ = P.emit("dve", TT(tB, qeB, si, ALU.mult),
                                 waits=[t_sin, t_rtmp_free[1]])
                    w3_ = P.emit("dve", TT(qrA, tA, tB, ALU.subtract),
                                 waits=[w1_, w2_, t_qr_free[qb_]])
                    w4_ = P.emit("po", TT(tC, qeB, co, ALU.mult),
                                 waits=[t_qev, t_cos, t_rtmp2_free[0]])
                    w5_ = P.emit("po", TT(tD, qeA, si, ALU.mult),
                                 waits=[t_rtmp2_free[1]])
                    w6_ = P.emit("po", TT(qrB, tC, tD, ALU.add),
                                 waits=[w4_, w5_, t_qr_free[qb_]])
                    t_qe_free[lt % 2] = [w2_, w5_]
                    t_rtmp_free = [w3_, w3_]
                    t_rtmp2_free = [w6_, w6_]
                    t_sq2 = P.emit("act", (lambda qb2: lambda e: e.activation(
                        sqf[:, :], qr[qb2][:, :], AF.Square))(qb_),
                        waits=[w3_, w6_, t_sqf_free])
                    sqr = sqf[:, :].rearrange("p (n j) -> p n j", j=64)
                    t_red = P.emit("dve", (lambda sr: lambda e: e.tensor_reduce(
                        nrm8[:, 0:8], sr, AX.X, ALU.add))(sqr),
                        waits=[t_sq2, t_nrm8_free])
                    t_sqf_free = t_red
                    nacc = qn2m if mat == "q" else kn2m
                    t_nx = P.emit("dve", (lambda na, occ: lambda e: e.tensor_tensor(
                        na[:, occ * 8:occ * 8 + 8], na[:, occ * 8:occ * 8 + 8],
                        nrm8[:, 0:8], ALU.max))(nacc, oc),
                        waits=[t_red, t_norm_acc[mat], t_qn0, t_kn0])
                    t_nrm8_free = t_nx
                    t_norm_acc[mat] = t_nx
                    dst_sc = q_sc if mat == "q" else k_sc
                    t_st = P.dma("sw", dst_sc.ap()[lt * 128:(lt + 1) * 128,
                                                   oc * 512:(oc + 1) * 512],
                                 qr[qb_][:, :], waits=[w3_, w6_])
                    t_qr_free[qb_] = [t_st, t_sq2]
                else:
                    vdst = bass.AP(VBUF, lt * (NH * 65) + oc * 8 * 65,
                                   [[LT * NH * 65, 128], [65, 8], [1, 64]])
                    t_vev = P.emit("dve", (lambda pp, vd: lambda e: e.tensor_copy(
                        vd, pp[:, :]))(ps, vdst), waits=[t_mm])
                    t_ps_free[myps] = t_vev
            t_wXT_free[wb] = t_last_mm
            t_bias[bsl] = t_last_mm
            gi = "qkv".index(mat) * OC + oc
            if gi < 2:
                emit_gamma_some(30)
            elif gi == 2:
                emit_gamma_some(10 ** 9)
                emit_p2()
                p2_done[0] = True
            else:
                emit_quant_some(10)
    for lt in range(LT):
        vd = bass.AP(VBUF, lt * (NH * 65) + 64, [[LT * NH * 65, 128], [65, NH], [1, 1]])
        t_vb_done = P.emit("dve", (lambda v: lambda e: e.memset(v, 1.0))(vd))

    es3n = contextlib.ExitStack()
    pnm1 = PSUM("pnm1", [NH, 128], BF16, es3n)
    pnm2 = PSUM("pnm2", [1, NH], BF16, es3n)
    pnm3 = PSUM("pnm3", [128, NH], F32, es3n)
    nq_b = SB("nq_b", [128, NH], BF16, es3n)
    nk_b = SB("nk_b", [128, NH], BF16, es3n)
    mx = SB("mx", [NH, 4], F32, es3n)
    nmb = SB("nmb", [NH, 1], BF16, es3n)
    nrow = SB("nrow", [1, NH], BF16, es3n)
    t_cq = P.emit("dve", lambda e: e.tensor_copy(nq_b[:, :], qn2m[:, :]),
                  waits=[t_norm_acc["q"]])
    t_ck = P.emit("dve", lambda e: e.tensor_copy(nk_b[:, :], kn2m[:, :]),
                  waits=[t_norm_acc["k"]])
    t_pn_free = None
    t_sq_toks = []
    for i, (nb, tn) in enumerate(((nq_b, t_cq), (nk_b, t_ck))):
        t_tr1 = P.emit("pe", (lambda nbb: lambda e: e.transpose(
            pnm1[:, :], nbb[:, :], id_sb[:, :]))(nb), waits=[tn, t_id, t_pn_free])
        t_mx = P.emit("dve", (lambda ii: lambda e: e.tensor_reduce(
            mx[:, ii:ii + 1], pnm1[:, :], AX.X, ALU.max))(i), waits=[t_tr1])
        t_pn_free = t_mx
        t_sqr = P.emit("act", (lambda ii: lambda e: e.activation(
            mx[:, 2 + ii:3 + ii], mx[:, ii:ii + 1], AF.Sqrt))(i), waits=[t_mx])
        t_sq_toks.append(t_sqr)
    t_mm_ = P.emit("dve", lambda e: e.tensor_mul(mx[:, 0:1], mx[:, 2:3], mx[:, 3:4]),
                   waits=t_sq_toks)
    t_nm = P.emit("dve", lambda e: e.tensor_scalar_mul(mx[:, 1:2], mx[:, 0:1], -0.125),
                  waits=[t_mm_])
    t_nmb = P.emit("dve", lambda e: e.tensor_copy(nmb[:, :], mx[:, 1:2]), waits=[t_nm])
    t_tr2 = P.emit("pe", lambda e: e.transpose(pnm2[:, :], nmb[:, :],
                                               id_sb[0:NH, 0:NH]), waits=[t_nmb])
    t_nrow = P.emit("dve", lambda e: e.tensor_copy(nrow[:, :], pnm2[:, :]),
                    waits=[t_tr2])
    t_bc = P.emit("pe", lambda e: e.matmul(pnm3[:, :], ones_r[:, :], nrow[:, :],
                                           start=True, stop=True),
                  waits=[t_nrow, t_ones])
    t_negm = P.emit("dve", lambda e: e.tensor_copy(negm[:, :], pnm3[:, :]),
                    waits=[t_bc])

    P.barrier()
    es3n.close()
    es3p.close()
    es1p.close()
    RA.reset()
    CTX = RA.take("CTX", [128, DT * L], BF16)
    RB.cur = vb_save
    expb = [RB.take(f"expb{i}", [128, 1024], BF16) for i in range(3)]
    RC.reset()
    H2 = H // 2
    rpb = [RC.take(f"rpb_{i}", [128, H2], BF16) for i in range(2)]
    rcs = RC.take("rcs", [128, 512], BF16)
    bcs = RC.take("bcs", [64, 512], BF16)
    ut_sb = RC.take("ut_sb", [128, 896], BF16)
    t_ut = P.dma("sy", ut_sb[:, :], ut_c.ap())
    qtb = [RC.take(f"qtb{i}", [128, L], BF16) for i in range(3)]
    ktb = [RC.take(f"ktb{i}", [128, L], BF16) for i in range(3)]
    W3W = H // 4
    w3ch_rel = RB.cur
    w3ch = RB.take("w3ch", [128, W3W], F32)
    w3qa = RB.take("w3qa", [128, W3W], BF16)
    w3qc = RB.take("w3qc", [128, W3W], BF16)
    HF3 = 2 if D >= 1024 else 1
    D2 = D // HF3
    rpb3 = [RB.take(f"rpb3_{i}", [128, D2], BF16) for i in range(2)]
    es4p = contextlib.ExitStack()
    psS = [PSUM(f"psS{i}", [128, 1024], F32, es4p) for i in range(3)]
    psc = PSUM("psc", [65, 512], F32, es4p)
    psbc = PSUM("psbc", [64, 512], F32, es4p)

    w3jobs = [("q", r0, c0) for r0 in range(0, D, 128)
              for c0 in range(0, H, H // 4)] + [
        ("r", ht, hf) for ht in range(HT) for hf in range(HF3)]
    w3_i = 0
    w3r_i = 0
    t_w3ch_free = None
    t_w3q_done = None
    t_w3a_free = None
    t_w3c_free = None
    t_rp3_free = [None, None]
    t_w3T_all = [None]

    def emit_w3_some(k):
        nonlocal w3_i, w3r_i, t_w3ch_free, t_w3q_done, t_w3a_free, t_w3c_free
        for _ in range(k):
            if w3_i >= len(w3jobs):
                return
            kind, idx, cc0 = w3jobs[w3_i]
            if kind == "r" and t_w3q_done is None:
                pass
            if kind == "q":
                tl = P.dma("sq", w3ch[:, :], w3.ap()[idx:idx + 128, cc0:cc0 + H // 4],
                           waits=[t_w3ch_free])
                ts1 = P.emit("act", lambda e: e.activation(
                    w3qa[:, :], w3ch[:, :], AF.Sign, bias=nthv[:, 2:3]),
                    waits=[tl, t_thv[2][1], t_w3a_free])
                tb = P.emit("dve", lambda e: e.tensor_scalar(
                    w3qc[:, :], w3ch[:, :], nthv[:, 2:3], 2.0,
                    ALU.is_ge, ALU.mult),
                    waits=[tl, t_thv[2][1], t_w3c_free])
                t_w3ch_free = [ts1, tb]
                tq = P.emit("dve", lambda e: e.scalar_tensor_tensor(
                    w3qc[:, :], w3qc[:, :], -1.0, w3qa[:, :],
                    ALU.add, ALU.add), waits=[ts1, tb])
                t_w3a_free = tq
                tw = P.dma("sw", w3q.ap()[idx:idx + 128, cc0:cc0 + H // 4],
                           w3qc[:, :], waits=[tq])
                t_w3c_free = tw
                if w3_i + 1 < len(w3jobs) and w3jobs[w3_i + 1][0] == "r":
                    t_w3q_done = P.qtok("sw")
            else:
                ht = idx
                hf = cc0
                b3 = w3r_i % 2
                w3r_i += 1
                tl = P.dma("sy", rpb3[b3][:, 0:D2],
                           w3q.ap()[hf * D2:(hf + 1) * D2,
                                    ht * 128:(ht + 1) * 128],
                           waits=[t_rp3_free[b3], t_w3q_done], transpose=True)
                dst = bass.AP(w3qT, ht * 128 * 512 + hf * (D2 // 512) * (H * 512),
                              [[512, 128], [H * 512, D2 // 512], [1, 512]])
                src = bass.AP(rpb3[b3], 0, [[D2, 128], [512, D2 // 512], [1, 512]])
                tw = P.dma("sw", dst, src, waits=[tl])
                t_rp3_free[b3] = tw
            w3_i += 1
            if w3_i == len(w3jobs):
                t_w3T_all[0] = P.qtok("sw")

    repack_jobs = []
    for (qsrc, qdst, mi) in ((w1q, w1qT, 0), (w2q, w2qT, 1)):
        for dt in range(DT):
            for hf in range(2):
                repack_jobs.append((qsrc, qdst, mi, dt, hf))
    rp_i = 0
    t_rpb_free = [None, None]
    t_rp_done = [None, None]

    def emit_repack_some(k):
        nonlocal rp_i
        for _ in range(k):
            if rp_i >= len(repack_jobs):
                return
            qsrc, qdst, mi, dt, hf = repack_jobs[rp_i]
            b = rp_i % 2
            tl = P.dma("sy", rpb[b][:, 0:H2],
                       qsrc.ap()[hf * H2:(hf + 1) * H2,
                                 dt * 128:(dt + 1) * 128],
                       waits=[t_rpb_free[b], t_qdone[mi]], transpose=True)
            dst = bass.AP(qdst, dt * 128 * 256 + hf * (H2 // 256) * (D * 256),
                          [[256, 128], [D * 256, H2 // 256], [1, 256]])
            src = bass.AP(rpb[b], 0, [[H2, 128], [256, H2 // 256], [1, 256]])
            tw = P.dma("sw", dst, src, waits=[tl])
            t_rpb_free[b] = tw
            if rp_i + 1 >= len(repack_jobs) or repack_jobs[rp_i + 1][2] != mi:
                t_rp_done[mi] = P.qtok("sw")
            rp_i += 1

    def emit_tail(k):
        left = k
        while left > 0:
            if gj_i < len(gamma_jobs):
                emit_gamma_some(min(left, len(gamma_jobs) - gj_i))
                left = 0
            elif not p2_done[0]:
                emit_p2()
                p2_done[0] = True
            elif qj_i < len(quant_jobs):
                n0 = qj_i
                emit_quant_some(left)
                left -= qj_i - n0
            elif rp_i < len(repack_jobs):
                n0 = rp_i
                emit_repack_some(left)
                left -= rp_i - n0
            elif w3_i < len(w3jobs):
                n0 = w3_i
                emit_w3_some(left)
                left -= w3_i - n0
            else:
                return

    t_qt = [None, None, None]
    t_kt = [None, None, None]
    t_qtb_free = [None, None, None]
    t_ktb_free = [None, None, None]
    last_sc = {}
    tasks = []
    for pm in range(NH // 2):
        for ni in range(2):
            n = 2 * pm + ni
            for qb in range(QB):
                nkt = (qb + 1) * 4
                for kt0 in range(0, nkt, 2):
                    tasks.append((n, pm % 3, ni * 64, qb, kt0, 2,
                                  kt0 + 2 == nkt))
    NT = len(tasks)
    t_exp_t = [None] * NT
    t_mask_t = [None] * NT
    t_sc_t = [None] * NT
    t_buf_prev = [[None, None] for _ in range(3)]
    t_psc_free = None
    t_psbc_free = None
    t_rcs_free = None
    t_bcs_free = None
    t_ctx_done = None
    seen_pm = -1

    def emit_qk_loads(pm, w):
        hb2 = pm % 3
        t_qt[hb2] = P.dma("sy", qtb[hb2][:, :],
                          q_sc.ap()[:, pm * 128:(pm + 1) * 128],
                          waits=w, transpose=True)
        t_kt[hb2] = P.dma("sy", ktb[hb2][:, :],
                          k_sc.ap()[:, pm * 128:(pm + 1) * 128],
                          waits=w, transpose=True)

    emit_qk_loads(0, [])

    def emit_scores_exp_mask(gi):
        nonlocal seen_pm, t_qt, t_kt
        (n, hb, rb, qb, g0, glen, lastq) = tasks[gi]
        pm = n // 2
        if pm > seen_pm:
            seen_pm = pm
            if pm >= 1:
                last_sc[pm - 1] = t_sc_t[gi - 1]
            if pm + 1 < NH // 2:
                emit_qk_loads(pm + 1, [last_sc.get(pm - 2)])
        bi = gi % 3
        pss = psS[bi]
        exb = expb[bi]
        prev_exp, prev_ctx = t_buf_prev[bi]
        t_sc = None
        for j in range(glen):
            kt = g0 + j
            t_sc = P.emit("pe", (lambda ps_, jj, ktt, hbb, qbb, rbb:
                                 lambda e: e.matmul(
                ps_[:, jj * 512:(jj + 1) * 512],
                ktb[hbb][rbb:rbb + 64, ktt * 128:ktt * 128 + 128],
                qtb[hbb][rbb:rbb + 64, qbb * 512:(qbb + 1) * 512],
                start=True, stop=True))(pss, j, kt, hb, qb, rb),
                waits=[t_qt[hb], t_kt[hb], prev_exp],
                n=(1 if j == glen - 1 else 0))
        t_sc_t[gi] = t_sc
        t_exp = P.emit("act", (lambda ps_, ex_, gl, nn: lambda e: e.activation(
            ex_[:, 0:gl * 512], ps_[:, 0:gl * 512], AF.Exp,
            bias=negm[:, nn:nn + 1], scale=0.125))(pss, exb, glen, n),
            waits=[t_sc, t_negm, prev_ctx])
        t_exp_t[gi] = t_exp
        t_m = t_exp
        for j in range(glen):
            kt = g0 + j
            dlt = kt * 128 - qb * 512
            if dlt >= 0:
                t_m = P.emit("dve", (lambda ex_, jj, dd: lambda e: e.tensor_tensor(
                    ex_[:, jj * 512:(jj + 1) * 512],
                    ex_[:, jj * 512:(jj + 1) * 512],
                    ut_sb[:, 384 - dd:896 - dd], ALU.mult))(exb, j, dlt),
                    waits=[t_exp, t_ut, t_m])
        t_mask_t[gi] = t_m

    def emit_ctx(gi):
        nonlocal t_psc_free, t_psbc_free, t_rcs_free, t_bcs_free, t_ctx_done
        nonlocal t_qtb_free, t_ktb_free
        (n, hb, rb, qb, g0, glen, lastq) = tasks[gi]
        bi = gi % 3
        exb = expb[bi]
        t_cm = None
        for j in range(glen):
            kt = g0 + j
            vsl = bass.AP(VBUF, kt * (NH * 65) + n * 65,
                          [[LT * NH * 65, 128], [1, 65]])
            w = [t_mask_t[gi], t_vb_done]
            if g0 == 0 and j == 0:
                w.append(t_psc_free)
            is_last = lastq and (j == glen - 1)
            t_cm = P.emit("pe", (lambda vs, ex_, jj, st, sp: lambda e: e.matmul(
                psc[:, :], vs, ex_[:, jj * 512:(jj + 1) * 512],
                start=st, stop=sp))(vsl, exb, j, g0 == 0 and j == 0, is_last),
                waits=w, n=(1 if is_last else 0))
        t_buf_prev[bi] = [t_exp_t[gi], t_cm]
        if lastq:
            t_rc = P.emit("dve", lambda e: e.reciprocal(
                rcs[64:65, :], psc[64:65, :]), waits=[t_cm, t_rcs_free])
            t_bc4 = P.emit("pe", lambda e: e.matmul(
                psbc[:, :], ones64[64:65, :], rcs[64:65, :],
                start=True, stop=True), waits=[t_rc, t_psbc_free, t_o64])
            t_rcs_free = t_bc4
            t_bce = P.emit("dve", lambda e: e.tensor_copy(
                bcs[:, :], psbc[:, :]), waits=[t_bc4, t_bcs_free])
            t_psbc_free = t_bce
            cdst = CTX[(n % 2) * 64:(n % 2) * 64 + 64,
                       (n // 2) * L + qb * 512:(n // 2) * L + (qb + 1) * 512]
            t_ce = P.emit("dve", (lambda cd: lambda e: e.tensor_tensor(
                cd, psc[0:64, :], bcs[:, :], ALU.mult))(cdst), waits=[t_bce])
            t_bcs_free = t_ce
            t_psc_free = t_ce
            t_ctx_done = t_ce
        if gi + 1 < NT and tasks[gi + 1][0] // 2 != n // 2:
            t_qtb_free[hb] = t_sc_t[gi]
            t_ktb_free[hb] = t_sc_t[gi]
        elif gi + 1 == NT:
            t_qtb_free[hb] = t_sc_t[gi]
            t_ktb_free[hb] = t_sc_t[gi]

    job_tick = 0
    for gi in range(NT + 2):
        if gi < NT:
            emit_scores_exp_mask(gi)
        if gi >= 2:
            emit_ctx(gi - 2)
        job_tick += 1
        if job_tick % 12 == 0:
            emit_tail(8)
    t_p4_pe = P.tok("pe")
    t_p4_act = P.tok("act")
    t_p4_dve = P.tok("dve")

    t_dbgc = None
    if dbg:
        t_dbgc = P.dma("sy", ctx_dbg.ap(), CTX[:, :], waits=[t_ctx_done])
    es4p.close()
    RB.cur = 0
    woT = [RB.take("woT0", [128, DT * 512], BF16), RB.take("woT1", [128, DT * 512], BF16)]
    stile = [RB.take("st0", [128, 512], BF16), RB.take("st1", [128, 512], BF16)]
    bias_sl5 = [RB.take("bias_sl0b", [1, 512], BF16), RB.take("bias_sl1b", [1, 512], BF16)]
    es5p = contextlib.ExitStack()
    ps5 = [PSUM("ps50", [128, 512], F32, es5p), PSUM("ps51", [128, 512], F32, es5p)]

    t_woT_free = [None, None]
    t_stile_free = [None, None]
    t_s_done = None
    t_ps5_free = [None, None]
    t_bsl5 = None
    for oc in range(OC):
        wb = oc % 2
        tws = []
        for dt in range(DT):
            tws.append(P.dma(
                "sy", woT[wb][:, dt * 512:(dt + 1) * 512],
                wbf["wo"].ap()[oc * 512:(oc + 1) * 512, dt * 128:(dt + 1) * 128],
                waits=[t_woT_free[wb], t_cast["wo"], t_p4_pe], transpose=True))
        bsl = oc % 2
        t_bsl5n = P.dma("sw", bias_sl5[bsl][0:1, :],
                        b_in["bo"].ap()[0:1, oc * 512:(oc + 1) * 512],
                        waits=[t_bsl5, t_p4_pe])
        t_last = None
        for lt in range(LT):
            pb5 = lt % 2
            ps = ps5[pb5]
            w = tws + [t_ps5_free[pb5], t_ctx_done, t_p4_act]
            for dt in range(DT):
                P.emit("pe", (lambda pp, dtt, ltt, wbb: lambda e: e.matmul(
                    pp[:, :], CTX[:, dtt * L + ltt * 128:dtt * L + ltt * 128 + 128],
                    woT[wbb][:, dtt * 512:(dtt + 1) * 512],
                    start=(dtt == 0), stop=False))(ps, dt, lt, wb),
                    waits=(w if dt == 0 else []), n=0)
            t_mm = P.emit("pe", (lambda pp, bb: lambda e: e.matmul(
                pp[:, :], ones_r[:, :], bias_sl5[bb][0:1, :],
                start=False, stop=True))(ps, bsl), waits=[t_ones, t_bsl5n])
            t_last = t_mm
            t_se = P.emit("dve", (lambda pp, sb2: lambda e: e.tensor_copy(
                sb2[:, :], pp[:, :]))(ps, stile[pb5]),
                waits=[t_mm, t_stile_free[pb5]])
            t_ps5_free[pb5] = t_se
            t_sw = P.dma("sq", s_sc.ap()[lt * 128:(lt + 1) * 128,
                                         oc * 512:(oc + 1) * 512],
                         stile[pb5][:, :], waits=[t_se])
            t_stile_free[pb5] = t_sw
            t_s_done = t_sw
            emit_tail(3)
        t_woT_free[wb] = t_last
        t_bsl5 = t_last
    t_s_done = P.qtok("sq")
    t_p5_pe = P.tok("pe")
    t_p5_dve = P.tok("dve")
    while qj_i < len(quant_jobs) or rp_i < len(repack_jobs):
        emit_tail(8)

    es5p.close()
    RA.reset()
    RB.cur = 0
    RC.reset()
    XNT2 = RA.take("XNT2", [128, DT * L], BF16)
    HIDT = RB.take("HIDT", [128, HT * 1024], BF16)
    w12b = [RC.take(f"w12b{i}", [128, DT * 256], BF16) for i in range(3)]
    LBS = 2 if L > 1024 else 1
    LBL = L // LBS
    t_xnt2 = []
    for dt in range(DT):
        t_xnt2.append(P.dma("sy", XNT2[:, dt * L:(dt + 1) * L],
                            xnT_sc.ap()[dt * 128:(dt + 1) * 128, :],
                            waits=t_xnt_w + [t_p5_pe]
                            + ([t_dbgc] if dbg else [])))
    sgb = RC.take("sgb", [128, 256], BF16)
    x2b = RC.take("x2b", [128, 256], BF16)
    RC.cur = 0
    w3b = [RC.take(f"w3b_{i}", [128, HT * 256], BF16) for i in range(2)]
    RC.cur = max(RC.cur, 3 * ((DT * 512 + 63) & ~63) + 2 * 512)
    gb6 = [RC.take("gb60", [128, 256], BF16), RC.take("gb61", [128, 256], BF16)]
    sob = [RC.take("sob0", [128, 256], BF16), RC.take("sob1", [128, 256], BF16)]
    xob = [RC.take("xob0", [128, 256], F32), RC.take("xob1", [128, 256], F32)]
    sgf6 = [RC.take("sgf60", [128, 256], F32), RC.take("sgf61", [128, 256], F32)]
    t_w3b_free = [None, None]
    t_p63_free = [None, None]
    pend_ost = [None]
    t_gb6_free = [None, None]
    t_xob_free = [None, None]
    t_sob_free = [None, None]
    t_sgf6_free = [None, None]
    t_hid_free = t_s_done + [t_p5_dve]
    t_wlast = None
    NHC = H // 256
    for lb in range(LBS):
        es6ap = contextlib.ExitStack()
        p61 = [PSUM(f"p61{lb}_{i}", [128, 512], F32, es6ap) for i in range(2)]
        p62 = [PSUM(f"p62{lb}_{i}", [128, 512], F32, es6ap) for i in range(2)]
        t_w12_free = [None, None, None]
        t_p61_free = [None, None]
        t_p62_free = [None, None]
        t_sgb_free = None
        t_x2b_free = None
        t_hid_done = None
        bi = 0
        for hc in range(NHC):
            if lb == 0 and (hc * 2 + 2) * 2048 > w3ch_rel:
                while w3_i < len(w3jobs):
                    emit_tail(8)
            b1 = bi % 3
            b2 = (bi + 1) % 3
            bi += 2
            in1 = bass.AP(w1qT, hc * (D * 256), [[256, 128], [128 * 256, DT], [1, 256]])
            in2 = bass.AP(w2qT, hc * (D * 256), [[256, 128], [128 * 256, DT], [1, 256]])
            tl1 = P.dma("sy", w12b[b1][:, :], in1,
                        waits=[t_w12_free[b1], t_rp_done[0], t_rp_done[1],
                               t_p4_pe, t_p4_dve, t_w3b_free[0],
                               t_w3b_free[1]])
            tl2 = P.dma("sy", w12b[b2][:, :], in2,
                        waits=[t_w12_free[b2], t_rp_done[0], t_rp_done[1],
                               t_p4_pe, t_p4_dve, t_w3b_free[0],
                               t_w3b_free[1]])
            t_l1 = None
            t_l2 = None
            for htl in range(2):
                ht = hc * 2 + htl
                for ls in range(LBL // 512):
                    lo = lb * LBL + ls * 512
                    lsp = ls % 2
                    pa = p61[lsp]
                    pc = p62[lsp]
                    for dt in range(DT):
                        P.emit("pe", (lambda pp, dtt, b_, ht_, lo_:
                                      lambda e: e.matmul(
                            pp[:, :],
                            w12b[b_][:, dtt * 256 + ht_ * 128:dtt * 256 + ht_ * 128 + 128],
                            XNT2[:, dtt * L + lo_:dtt * L + lo_ + 512],
                            start=(dtt == 0), stop=(dtt == DT - 1)))(pa, dt, b1, htl, lo),
                            waits=(([tl1, t_p61_free[lsp], t_p5_dve,
                                     t_p4_act] + t_xnt2)
                                   if dt == 0 else []),
                            n=(1 if dt == DT - 1 else 0))
                    t_l1 = P.tok("pe")
                    for dt in range(DT):
                        P.emit("pe", (lambda pp, dtt, b_, ht_, lo_:
                                      lambda e: e.matmul(
                            pp[:, :],
                            w12b[b_][:, dtt * 256 + ht_ * 128:dtt * 256 + ht_ * 128 + 128],
                            XNT2[:, dtt * L + lo_:dtt * L + lo_ + 512],
                            start=(dtt == 0), stop=(dtt == DT - 1)))(pc, dt, b2, htl, lo),
                            waits=([tl2, t_p62_free[lsp], t_p5_dve,
                                    t_p4_act] if dt == 0 else []),
                            n=(1 if dt == DT - 1 else 0))
                    t_l2 = P.tok("pe")
                    for sub in range(2):
                        t_sg = P.emit("act", (lambda pp, ss: lambda e: e.activation(
                            sgb[:, :], pp[:, ss * 256:ss * 256 + 256], AF.Sigmoid,
                            scale=0.5))(pa, sub), waits=[t_l1, t_sgb_free])
                        t_x2 = P.emit("dve", (lambda pp, ss: lambda e: e.tensor_scalar_mul(
                            x2b[:, :], pp[:, ss * 256:ss * 256 + 256], 0.5))(pc, sub),
                            waits=[t_l2, t_x2b_free])
                        t_hd = P.emit("dve", (lambda ht_, ls_, ss: lambda e: e.tensor_mul(
                            HIDT[:, ht_ * 1024 + ls_ * 512 + ss * 256:
                                 ht_ * 1024 + ls_ * 512 + ss * 256 + 256],
                            sgb[:, :], x2b[:, :]))(ht, ls, sub),
                            waits=[t_sg, t_x2, t_hid_free,
                                   (t_w3T_all[0]
                                    if (ht + 1) * 2048 > w3ch_rel
                                    else None)])
                        t_sgb_free = t_hd
                        t_x2b_free = t_hd
                        t_hid_done = t_hd
                    t_p61_free[lsp] = t_sgb_free
                    t_p62_free[lsp] = t_x2b_free
            t_w12_free[b1] = t_l1
            t_w12_free[b2] = t_l2
            emit_tail(4)
        while w3_i < len(w3jobs):
            emit_tail(8)
        es6ap.close()
        es6bp = contextlib.ExitStack()
        p63 = [PSUM(f"p63{lb}_{i}", [128, 2048], F32, es6bp) for i in range(2)]
        NLB = LBL // 128
        HT2 = HT // 2
        for ob2 in range(D // 512):
            tl_h = [None, None]
            for h in range(2):
                in3 = bass.AP(w3qT, ob2 * (H * 512) + h * HT2 * 128 * 512,
                              [[512, 128], [128 * 512, HT2], [1, 512]])
                tl_h[h] = P.dma("sy", w3b[h][:, :], in3,
                                waits=[t_w3b_free[h], t_w3T_all[0], t_hid_done,
                                       t_w12_free[0], t_w12_free[1],
                                       t_w12_free[2]])
            for ltq in range(NLB // 4):
                pq = ltq % 2
                ps = p63[pq]
                for h in range(2):
                    for htl in range(HT2):
                        ht = h * HT2 + htl
                        for lt4 in range(4):
                            ltl = ltq * 4 + lt4
                            P.emit("pe", (lambda pp, ht_, ltl_, l4, hh, hl:
                                          lambda e: e.matmul(
                                pp[:, l4 * 512:(l4 + 1) * 512],
                                HIDT[:, ht_ * 1024 + ltl_ * 128:
                                     ht_ * 1024 + ltl_ * 128 + 128],
                                w3b[hh][:, hl * 512:(hl + 1) * 512],
                                start=(ht_ == 0), stop=(ht_ == HT - 1)))(
                                    ps, ht, ltl, lt4, h, htl),
                                waits=(([tl_h[h], t_p63_free[pq]])
                                       if htl == 0 and lt4 == 0 else []),
                                n=(1 if ht == HT - 1 and lt4 == 3 else 0))
                t_wlast = P.tok("pe")
                if ltq == NLB // 4 - 1:
                    t_w3b_free[0] = t_wlast
                    t_w3b_free[1] = t_wlast
                t_g = None
                for lt4 in range(4):
                    ltl = ltq * 4 + lt4
                    lo = lb * LBL + ltl * 128
                    for ss in range(2):
                        ob = ob2 * 2 + ss
                        xb_ = (ltl * 2 + ss) % 2
                        t_xl = P.dma("sy", xob[xb_][:, :],
                                     x_in.ap()[lo:lo + 128,
                                               ob * 256:(ob + 1) * 256],
                                     waits=[t_xob_free[xb_], t_p4_pe])
                        t_sl = P.dma("sw", sob[xb_][:, :],
                                     s_sc.ap()[lo:lo + 128,
                                               ob * 256:(ob + 1) * 256],
                                     waits=[t_sob_free[xb_], t_s_done,
                                            t_p4_pe])
                        if pend_ost[0] is not None:
                            po_, pou, pxb = pend_ost[0]
                            t_od = P.dma("sw", po_, xob[pxb][:, :], waits=[pou])
                            t_xob_free[pxb] = t_od
                            pend_ost[0] = None
                        t_g = P.emit("act", (lambda pp, l4, s_, g_: lambda e:
                                             e.activation(
                            gb6[g_][:, :], pp[:, l4 * 512 + s_ * 256:
                                          l4 * 512 + s_ * 256 + 256], AF.Silu,
                            scale=0.5))(ps, lt4, ss, xb_),
                            waits=[t_wlast, t_gb6_free[xb_]])
                        t_sg6 = P.emit("dve", (lambda xb2: lambda e: e.tensor_mul(
                            sgf6[xb2][:, :], gb6[xb2][:, :], sob[xb2][:, :]))(xb_),
                            waits=[t_g, t_sl, t_sgf6_free[xb_]])
                        t_gb6_free[xb_] = t_sg6
                        t_sob_free[xb_] = t_sg6
                        t_ou = P.emit("dve", (lambda xb2: lambda e: e.tensor_add(
                            xob[xb2][:, :], sgf6[xb2][:, :], xob[xb2][:, :]))(xb_),
                            waits=[t_sg6, t_xl])
                        t_sgf6_free[xb_] = t_ou
                        pend_ost[0] = (out_t.ap()[lo:lo + 128,
                                                  ob * 256:(ob + 1) * 256],
                                       t_ou, xb_)
                t_p63_free[pq] = t_g
        t_hid_free = t_wlast
        es6bp.close()
    if pend_ost[0] is not None:
        po_, pou, pxb = pend_ost[0]
        t_od = P.dma("sw", po_, xob[pxb][:, :], waits=[pou])
        t_xob_free[pxb] = t_od
        pend_ost[0] = None
    P.emit("po", lambda e: e.nop(), waits=P.qtok("sw") + P.qtok("sq"))

    with nc.allow_low_precision(reason="bf16 transposes and bf16 staging by design"), \
         nc.Block() as block:
        @block.sync
        def _(e):
            P.replay("sy", e)

        @block.scalar
        def _(e):
            P.replay("act", e)

        @block.tensor
        def _(e):
            P.replay("pe", e)

        @block.vector
        def _(e):
            P.replay("dve", e)

        @block.gpsimd
        def _(e):
            P.replay("po", e)

    return nc


def make_consts(cfg: Cfg):
    L, NH, LT = cfg.L, cfg.NH, cfg.LT
    pos = np.arange(L, dtype=np.float64)
    ts = 10000.0 ** (np.arange(NH, dtype=np.float64) / 32.0)
    ang = pos[:, None] / ts[None, :]
    cos = np.cos(ang).astype(np.float32)
    sin = np.sin(ang).astype(np.float32)
    cos_c = np.ascontiguousarray(
        cos.reshape(LT, 128, NH).transpose(1, 0, 2).reshape(128, LT * NH)
    ).astype(ml_dtypes.bfloat16)
    sin_c = np.ascontiguousarray(
        sin.reshape(LT, 128, NH).transpose(1, 0, 2).reshape(128, LT * NH)
    ).astype(ml_dtypes.bfloat16)
    ut = (np.arange(896)[None, :] >= 384 + np.arange(128)[:, None]).astype(ml_dtypes.bfloat16)
    ident = np.eye(128, dtype=ml_dtypes.bfloat16)
    return cos_c, sin_c, ut, ident


def make_in_maps(cfg, x_all, wq_, bq_, wk_, bk_, wv_, bv_, wo_, bo_, w1_, w2_, w3_,
                 n_cores):
    cos_c, sin_c, ut, ident = make_consts(cfg)
    common = {
        "wq": np.ascontiguousarray(wq_, np.float32),
        "wk": np.ascontiguousarray(wk_, np.float32),
        "wv": np.ascontiguousarray(wv_, np.float32),
        "wo": np.ascontiguousarray(wo_, np.float32),
        "bq": np.ascontiguousarray(bq_, np.float32).reshape(1, -1),
        "bk": np.ascontiguousarray(bk_, np.float32).reshape(1, -1),
        "bv": np.ascontiguousarray(bv_, np.float32).reshape(1, -1),
        "bo": np.ascontiguousarray(bo_, np.float32).reshape(1, -1),
        "w1": np.ascontiguousarray(w1_, np.float32),
        "w2": np.ascontiguousarray(w2_, np.float32),
        "w3": np.ascontiguousarray(w3_, np.float32),
        "cos_c": cos_c, "sin_c": sin_c, "ut_c": ut, "id_c": ident,
    }
    xs = np.ascontiguousarray(np.asarray(x_all, np.float32))
    in_maps = []
    for b in range(n_cores):
        m = dict(common)
        m["x"] = np.ascontiguousarray(xs[b])
        in_maps.append(m)
    return in_maps


def kernel(L0_attn_x, L0_attn_wq_w, L0_attn_wq_b, L0_attn_wk_w, L0_attn_wk_b,
           L0_attn_wv_w, L0_attn_wv_b, L0_attn_wo_w, L0_attn_wo_b,
           L0_ffn_w1, L0_ffn_w2, L0_ffn_w3):
    cfg = Cfg()
    nc = build_nc(cfg)
    in_maps = make_in_maps(cfg, L0_attn_x, L0_attn_wq_w, L0_attn_wq_b,
                           L0_attn_wk_w, L0_attn_wk_b, L0_attn_wv_w, L0_attn_wv_b,
                           L0_attn_wo_w, L0_attn_wo_b, L0_ffn_w1, L0_ffn_w2,
                           L0_ffn_w3, 8)
    res = run_bass_kernel_spmd(nc, in_maps, core_ids=list(range(8)))
    return np.stack([np.asarray(res.results[b]["out"], np.float32) for b in range(8)])
